# version 07
# baseline (speedup 1.0000x reference)
import numpy as np
import ml_dtypes
import concourse.bass as bass
import concourse.mybir as mybir
from concourse.bass_utils import run_bass_kernel_spmd

F32 = mybir.dt.float32
BF16 = mybir.dt.bfloat16
AF = mybir.ActivationFunctionType
OP = mybir.AluOpType
AX = mybir.AxisListType

L = 2048
DM = 1024
E2 = 2048
NST = 16
TC = 256
NCH = L // TC
ECT = 16
NEXP = 8
DBG = False
AR_W = 35648
INTERLEAVE = True


class Prog:
    ENGINES = ("tensor", "vector", "scalar", "sync", "gpsimd")
    QUEUES = {"qs": "sync", "qx": "scalar", "qg": "gpsimd"}

    def __init__(self):
        self.ops = {e: [] for e in self.ENGINES}
        self.lanes = {}
        self.buf = {}

    def _deps(self, reads, writes):
        deps = []
        for b in reads:
            st = self.buf.get(b)
            if st and st[0] is not None:
                deps.append(st[0])
        for b in writes:
            st = self.buf.get(b)
            if st:
                if st[0] is not None:
                    deps.append(st[0])
                deps.extend(st[1])
        return deps

    def _update(self, h, reads, writes):
        for b in writes:
            self.buf[b] = [h, []]
        for b in reads:
            st = self.buf.setdefault(b, [None, []])
            st[1].append(h)

    def op(self, eng, fn, reads=(), writes=(), deps=(), strict=False):
        dl = list(deps) + self._deps(reads, writes)
        h = ("e", eng, len(self.ops[eng]))
        self.ops[eng].append({"fn": fn, "deps": dl, "q": None, "strict": strict})
        self._update(h, reads, writes)
        return h

    def dma(self, queue, lane, fn, reads=(), writes=(), deps=(), pipe=False):
        dl = list(deps) + self._deps(reads, writes)
        n = self.lanes.get(lane, 0) + 1
        self.lanes[lane] = n
        h = ("q", lane, n)
        if n > 1 and not pipe:
            dl.append(("q", lane, n - 1))
        self.ops[self.QUEUES[queue]].append({"fn": fn, "deps": dl, "q": lane,
                                             "strict": False})
        self._update(h, reads, writes)
        return h

    def fence(self, eng, names):
        return self.op(eng, lambda e: e.engine_nop(), reads=(), writes=tuple(names))

    def emit(self, nc, sems, mksem):
        qsems = {ln: mksem("q_" + ln) for ln in self.lanes}
        flagged = {e: set() for e in self.ENGINES}
        for eng in self.ENGINES:
            for o in self.ops[eng]:
                for d in o["deps"]:
                    if d[0] == "e" and (d[1] != eng or o["strict"]):
                        flagged[d[1]].add(d[2])
        semval = {}
        for eng in self.ENGINES:
            c = 0
            vals = []
            for i in range(len(self.ops[eng])):
                if i in flagged[eng]:
                    c += 1
                vals.append(c)
            semval[eng] = vals

        def make_body(eng):
            ops = self.ops[eng]
            fl = flagged[eng]

            def body(e):
                hw = {}
                for i, o in enumerate(ops):
                    for d in o["deps"]:
                        if d[0] == "e":
                            if d[1] == eng and not o["strict"]:
                                continue
                            key, val, sem = d[1], semval[d[1]][d[2]], sems[d[1]]
                        else:
                            key, val, sem = d[1], 16 * d[2], qsems[d[1]]
                        if hw.get(key, -1) < val:
                            e.wait_ge(sem, val)
                            hw[key] = val
                    instr = o["fn"](e)
                    if o["q"] is not None:
                        instr.then_inc(qsems[o["q"]], 16)
                    elif i in fl:
                        instr.then_inc(sems[eng], 1)

            return body

        with nc.Block() as block:
            block.tensor(make_body("tensor"))
            block.vector(make_body("vector"))
            block.scalar(make_body("scalar"))
            block.sync(make_body("sync"))
            block.gpsimd(make_body("gpsimd"))


def build_nc(pmask=(1, 1, 1, 4, 1)):
    nc = bass.Bass("TRN2", target_bir_lowering=False, detect_race_conditions=False)
    p = Prog()

    def ein(name, shape, dt=F32):
        return nc.dram_tensor(name, list(shape), dt, kind="ExternalInput").ap()

    xb = ein("xb", (L, DM))
    inT = {0: ein("inT_f", (32, 128, 1024)), 1: ein("inT_b", (32, 128, 1024))}
    outT = {0: ein("outT_f", (8, 128, 2048)), 1: ein("outT_b", (8, 128, 2048))}
    mergeT2 = ein("mergeT2", (2, 8, 128, 1024))
    xpT = {0: ein("xpT_f", (128, 1536)), 1: ein("xpT_b", (128, 1536))}
    dtwT = {0: ein("dtwT_f", (64, 2048)), 1: ein("dtwT_b", (64, 2048))}
    convw = {0: ein("convw_f", (128, 64)), 1: ein("convw_b", (128, 64))}
    convb = {0: ein("convb_f", (128, 16)), 1: ein("convb_b", (128, 16))}
    dtb = {0: ein("dtb_f", (128, 16)), 1: ein("dtb_b", (128, 16))}
    Dparm = {0: ein("D_f", (128, 16)), 1: ein("D_b", (128, 16))}
    Alog = {0: ein("A_f", (128, 256)), 1: ein("A_b", (128, 256))}
    rw = ein("rw", (128, 64))
    rb = ein("rb", (1, 8))
    w1p = ein("w1p", (NEXP, 32, 128, 1024))
    w2blk = ein("w2blk", (NEXP, 128, 32768))
    b1blk = ein("b1blk", (NEXP, 128, 32))
    b2 = ein("b2", (NEXP, 1024))
    ident = ein("ident", (128, 128))
    out_h = nc.dram_tensor("out", [L, DM], F32, kind="ExternalOutput").ap()

    def idr(name, shape, dt):
        kind = "ExternalOutput" if (DBG and name in ("attn_f", "attn_b", "ffn")) else "Internal"
        return nc.dram_tensor(name, list(shape), dt, kind=kind).ap()

    inTbf = {0: idr("inTbf_f", (32, 128, 1024), BF16), 1: idr("inTbf_b", (32, 128, 1024), BF16)}
    outTbf = {0: idr("outTbf_f", (8, 128, 2048), BF16), 1: idr("outTbf_b", (8, 128, 2048), BF16)}
    mergeTbf = idr("mergeTbf", (2, 8, 128, 1024), BF16)
    xpTbf = {0: idr("xpTbf_f", (128, 1536), BF16), 1: idr("xpTbf_b", (128, 1536), BF16)}
    dtwTbf = {0: idr("dtwTbf_f", (64, 2048), BF16), 1: idr("dtwTbf_b", (64, 2048), BF16)}
    w1bf = idr("w1bf", (NEXP, 32, 128, 1024), BF16)
    w2bf = idr("w2bf", (NEXP, 128, 32768), BF16)
    attn_d = {0: idr("attn_f", (L, DM), F32), 1: idr("attn_b", (L, DM), F32)}
    ffn_d = idr("ffn", (L, DM), F32)
    bcr = [idr("bcr0", (32, TC), F32), idr("bcr1", (32, TC), F32)]
    dbg = nc.dram_tensor("dbg", [8, 128, 4096], F32,
                         kind="ExternalOutput" if DBG else "Internal").ap()
    dbg2 = nc.dram_tensor("dbg2", [128, 16384], BF16,
                          kind="ExternalOutput" if DBG else "Internal").ap()

    def dump(slot, src_ap, rd, w=4096):
        if DBG:
            p.dma("qx", "dbg", lambda e, s=src_ap, t=dbg[slot][:, 0:w]: e.dma_start(out=t, in_=s),
                  reads=rd, writes=())

    from contextlib import ExitStack
    stack = ExitStack()
    e_ = stack.enter_context
    AR = e_(nc.sbuf_tensor("AR", [128, AR_W], F32))
    XN = e_(nc.sbuf_tensor("XN", [128, 16384], BF16))
    PARS = e_(nc.sbuf_tensor("PARS", [128, 1664], F32))
    LOW = e_(nc.sbuf_tensor("LOW", [128, 2048], F32))
    IDE = e_(nc.sbuf_tensor("IDE", [128, 128], F32))
    WST = e_(nc.sbuf_tensor("WST", [128, 4096], BF16))
    BCB = e_(nc.sbuf_tensor("BCB", [128, 1024], F32))
    RWS = e_(nc.sbuf_tensor("RWS", [128, 64], F32))
    MOEW = e_(nc.sbuf_tensor("MOEW", [128, 4096], BF16))
    HP = e_(nc.sbuf_tensor("HP", [128, 512], BF16))
    PB = [e_(nc.psum_tensor("PB0", [128, 2048], F32)),
          e_(nc.psum_tensor("PB1", [128, 2048], F32))]

    sems = {en: e_(nc.semaphore("s_" + en)) for en in Prog.ENGINES}

    def ar(off, w):
        return AR[:, off:off + w]

    DTW = AR[:, 31296:32320].bitcast(BF16)
    XPT = AR[:, 32320:33088].bitcast(BF16)

    def ar3(off, outer_step, outer_cnt, inner_cnt, inner_step=1, inner_off=0):
        return bass.AP(AR, off + inner_off,
                       [[AR_W, 128], [outer_step, outer_cnt], [inner_step, inner_cnt]])

    def psr(i, r, w=256):
        return PB[i][:, r * 512: r * 512 + w]

    def prn(i, r, n=1):
        return tuple(f"p{i}r{r + k}" for k in range(n))

    O_XI, O_XC, O_SZ, O_DE, O_Y = 0, 4160, 8256, 12352, 16448
    O_HB, O_DBU = 20544, 22592
    O_DAS = (26688, 26944, 35136, 35392)
    O_XB = 27200
    O_FBF = 29248
    O_ST = 30272
    O_YSB = 33088
    P0_XT, P0_PC, P0_SC = 0, 16384, 17408
    PA_CONVW = {0: 0, 1: 64}
    PA_CONVB = {0: 128, 1: 144}
    PA_DTB = {0: 160, 1: 176}
    PA_D = {0: 192, 1: 208}
    PA_A = {0: 224, 1: 480}
    PA_ST = {0: 736, 1: 992}
    PA_STAT = 1248
    PA_LG = 1344
    PA_G = 1472
    PA_RB = 1600
    PA_TAIL = 1612
    LW_DBL = 0
    LW_B2 = 256
    LW_B2BF = 1280
    LW_DTBF = 1792
    LW_SC = 1920
    WS_S = [0, 1024]

    TEN, VEC, SCA, POOL = "tensor", "vector", "scalar", "gpsimd"

    def mm(out_ap, lhsT, rhs, start, stop, reads, writes, deps=()):
        return p.op(TEN, lambda e, a=out_ap, b=lhsT, c=rhs, s1=start, s2=stop:
                    e.matmul(a, b, c, start=s1, stop=s2),
                    reads=reads, writes=writes, deps=deps)

    def act(out, in_, func, bias, scale, reads, writes, accum=None, deps=(), strict=False):
        return p.op(SCA, lambda e, o=out, i=in_, f=func, b=bias, s=scale, au=accum:
                    e.activation(o, i, f, bias=b, scale=s, accum_out=au),
                    reads=reads, writes=writes, deps=deps, strict=strict)

    def flat2(ap):
        n = 1
        for s in ap.shape:
            n *= s
        assert n % 128 == 0
        return bass.AP(ap.tensor, 0, [[n // 128, 128], [1, n // 128]])

    for d in (0, 1):
        p.dma("qg", "cast0", lambda e, s=flat2(inT[d]), t=flat2(inTbf[d]): e.dma_start(out=t, in_=s),
              writes=(f"inTbf{d}",))
        p.dma("qg", "cast0", lambda e, s=flat2(outT[d]), t=flat2(outTbf[d]): e.dma_start(out=t, in_=s),
              writes=(f"outTbf{d}",))
        p.dma("qg", "cast0", lambda e, s=flat2(xpT[d]), t=flat2(xpTbf[d]): e.dma_start(out=t, in_=s),
              writes=(f"xpTbf{d}",))
        p.dma("qg", "cast0", lambda e, s=flat2(dtwT[d]), t=flat2(dtwTbf[d]): e.dma_start(out=t, in_=s),
              writes=(f"dtwTbf{d}",))
    p.dma("qg", "cast0", lambda e, s=flat2(mergeT2), t=flat2(mergeTbf): e.dma_start(out=t, in_=s),
          writes=("mergeTbf",))

    def flat2o(ap_tensor, off, n):
        return bass.AP(ap_tensor, off, [[n // 128, 128], [1, n // 128]])

    NW1 = 32 * 128 * 1024
    NW2 = 128 * 32768
    for j in range(NEXP):
        p.dma("qg", "cast1", lambda e, jj=j: e.dma_start(
            out=flat2o(w1bf.tensor, jj * NW1, NW1), in_=flat2o(w1p.tensor, jj * NW1, NW1)),
            writes=(f"w1bf{j}",))
        p.dma("qg", "cast1", lambda e, jj=j: e.dma_start(
            out=flat2o(w2bf.tensor, jj * NW2, NW2), in_=flat2o(w2blk.tensor, jj * NW2, NW2)),
            writes=(f"w2bf{j}",))

    p.dma("qx", "par", lambda e: e.dma_start(out=IDE[:], in_=ident), writes=("IDE",))
    for d in (0, 1):
        for src, off, wdt, nm in ((convw[d], PA_CONVW[d], 64, f"convw{d}"),
                                  (convb[d], PA_CONVB[d], 16, f"convb{d}"),
                                  (dtb[d], PA_DTB[d], 16, f"dtb{d}"),
                                  (Dparm[d], PA_D[d], 16, f"D{d}"),
                                  (Alog[d], PA_A[d], 256, f"A{d}")):
            p.dma("qx", "par", lambda e, s=src, o=off, w=wdt: e.dma_start(
                out=PARS[:, o:o + w], in_=s), writes=(nm,))
    p.dma("qx", "par", lambda e: e.dma_start(out=LOW[0:8, LW_B2:LW_B2 + 1024], in_=b2),
          writes=("b2f",))
    p.dma("qx", "par", lambda e: e.dma_start(
        out=PARS[:, PA_RB:PA_RB + 8], in_=bass.AP(rb.tensor, 0, [[0, 128], [1, 8]])),
        writes=("rbb",))
    p.dma("qx", "par", lambda e: e.dma_start(out=RWS[:], in_=rw), writes=("rw",))

    for d in (0, 1):
        Aap = PARS[:, PA_A[d]:PA_A[d] + 256]
        act(Aap, Aap, AF.Exp, 0.0, 1.0, reads=(f"A{d}",), writes=(f"A{d}",))
        p.op(VEC, lambda e, a=Aap: e.tensor_scalar_mul(a, a, -1.0),
             reads=(f"A{d}",), writes=(f"A{d}",))
    b2bfv = LOW[:, LW_B2BF:LW_B2BF + 512].bitcast(BF16)
    p.op(VEC, lambda e: e.tensor_copy(b2bfv[0:8, :], LOW[0:8, LW_B2:LW_B2 + 1024]),
         reads=("b2f",), writes=("b2bf",))

    if not pmask[0]:
        p.op(SCA, lambda e: e.nop(), reads=(), writes=())
        p.emit(nc, sems, qsems)
        stack.close()
        return nc
    stat = PARS[:, PA_STAT:PA_STAT + 96]
    for tt in range(16):
        xt = ar(P0_XT + tt * 1024, 1024)
        p.dma("qx", "xt", lambda e, t=xt, a=tt: e.dma_start(out=t, in_=xb[a * 128:(a + 1) * 128, :]),
              writes=(f"xt{tt}",))
        act(ar(P0_SC, 1024), xt, AF.Copy, 0.0, 1.0, reads=(f"xt{tt}",),
            writes=("p0sc", "stat_s"), accum=stat[:, tt:tt + 1])
        act(ar(P0_SC, 1024), xt, AF.Square, 0.0, 1.0, reads=(f"xt{tt}",),
            writes=("p0sc", "stat_s"), accum=stat[:, 16 + tt:17 + tt])
    p.op(VEC, lambda e: e.tensor_scalar_mul(stat[:, 32:48], stat[:, 0:16], 1.0 / DM),
         reads=("stat_s",), writes=("stat_m",), strict=True)
    p.op(VEC, lambda e: e.tensor_scalar_mul(stat[:, 48:64], stat[:, 16:32], 1.0 / DM),
         reads=("stat_s",), writes=("stat_e",), strict=True)
    act(stat[:, 64:80], stat[:, 32:48], AF.Square, 0.0, 1.0,
        reads=("stat_m",), writes=("stat_v",))
    p.op(VEC, lambda e: e.tensor_sub(stat[:, 64:80], stat[:, 48:64], stat[:, 64:80]),
         reads=("stat_e", "stat_v"), writes=("stat_v",), strict=True)
    p.op(VEC, lambda e: e.tensor_scalar_add(stat[:, 64:80], stat[:, 64:80], 1e-5),
         reads=("stat_v",), writes=("stat_v",), strict=True)
    act(stat[:, 64:80], stat[:, 64:80], AF.Sqrt, 0.0, 1.0, reads=("stat_v",), writes=("stat_v",))
    p.op(VEC, lambda e: e.reciprocal(stat[:, 80:96], stat[:, 64:80]),
         reads=("stat_v",), writes=("stat_r",), strict=True)

    logits = PARS[:, PA_LG:PA_LG + 128]
    gates = PARS[:, PA_G:PA_G + 128]
    for tt in range(16):
        xt = ar(P0_XT + tt * 1024, 1024)
        xnf = ar(P0_SC, 1024)
        p.op(VEC, lambda e, t=xt, o=xnf, c=tt: e.tensor_scalar(
            o, t, stat[:, 32 + c:33 + c], stat[:, 80 + c:81 + c], OP.subtract, OP.mult),
            reads=(f"xt{tt}", "stat_m", "stat_r"), writes=("p0sc",), strict=True)
        for dk in range(8):
            r = dk % 4
            pt = psr(0, r, 128)
            p.op(TEN, lambda e, o=pt, i=xnf[:, dk * 128:(dk + 1) * 128]:
                 e.transpose(o, i, IDE[:]),
                 reads=("p0sc", "IDE"), writes=prn(0, r))
            p.op(SCA, lambda e, o=XN[:, dk * 2048 + tt * 128: dk * 2048 + (tt + 1) * 128], i=pt:
                 e.activation(o, i, AF.Copy, bias=0.0, scale=1.0),
                 reads=prn(0, r), writes=("xn",))
            p.op(SCA, lambda e, o=ar(P0_PC + dk * 128, 128), i=pt:
                 e.activation(o, i, AF.Copy, bias=0.0, scale=1.0),
                 reads=prn(0, r), writes=(f"pc{dk}",))
        for dk in range(8):
            mm(PB[1][:, 0:8], ar(P0_PC + dk * 128, 128), RWS[:, dk * 8:(dk + 1) * 8],
               dk == 0, dk == 7, reads=(f"pc{dk}", "rw"), writes=prn(1, 0))
        p.op(VEC, lambda e, o=logits[:, tt * 8:(tt + 1) * 8]: e.tensor_add(
            o, PB[1][:, 0:8], PARS[:, PA_RB:PA_RB + 8]),
            reads=prn(1, 0) + ("rbb",), writes=(f"lg{tt}",), strict=True)
    sc = LOW[:, LW_SC:LW_SC + 128]
    for tt in range(16):
        lg = logits[:, tt * 8:(tt + 1) * 8]
        gt = gates[:, tt * 8:(tt + 1) * 8]
        m1, m2, ex, dn, iv, g2 = (sc[:, i:i + 1] for i in range(6))
        l2, mk1, mk2, t1 = sc[:, 8:16], sc[:, 16:24], sc[:, 24:32], sc[:, 32:40]
        p.op(VEC, lambda e, a=m1, b=lg: e.reduce_max(a, b, axis=AX.X),
             reads=(f"lg{tt}",), writes=("t2",), strict=True)
        p.op(VEC, lambda e, a=mk1, b=lg, c=m1: e.tensor_single_scalar(a, b, c, OP.is_ge),
             reads=(f"lg{tt}", "t2"), writes=("t2",), strict=True)
        p.op(VEC, lambda e, a=l2, b=mk1, c=lg: e.scalar_tensor_tensor(
            a, b, -1e30, c, OP.mult, OP.add), reads=(f"lg{tt}", "t2"), writes=("t2",), strict=True)
        p.op(VEC, lambda e, a=m2, b=l2: e.reduce_max(a, b, axis=AX.X),
             reads=("t2",), writes=("t2",), strict=True)
        p.op(VEC, lambda e, a=mk2, b=l2, c=m2: e.tensor_single_scalar(a, b, c, OP.is_ge),
             reads=("t2",), writes=("t2",), strict=True)
        p.op(VEC, lambda e, a=ex, b=m2, c=m1: e.tensor_sub(a, b, c),
             reads=("t2",), writes=("t2",), strict=True)
        act(ex, ex, AF.Exp, 0.0, 1.0, reads=("t2",), writes=("t2",), strict=True)
        p.op(VEC, lambda e, a=dn, b=ex: e.tensor_scalar_add(a, b, 1.0),
             reads=("t2",), writes=("t2",), strict=True)
        p.op(VEC, lambda e, a=iv, b=dn: e.reciprocal(a, b), reads=("t2",), writes=("t2",), strict=True)
        p.op(VEC, lambda e, a=g2, b=ex, c=iv: e.tensor_mul(a, b, c),
             reads=("t2",), writes=("t2",), strict=True)
        p.op(VEC, lambda e, a=t1, b=mk1, c=iv: e.tensor_single_scalar(a, b, c, OP.mult),
             reads=("t2",), writes=("t2",), strict=True)
        p.op(VEC, lambda e, a=gt, b=mk2, c=g2, d0=t1: e.scalar_tensor_tensor(
            a, b, c, d0, OP.mult, OP.add), reads=("t2",), writes=(f"g{tt}",), strict=True)

    p.fence(VEC, [f"xt{t}" for t in range(16)] + ["p0sc"] +
            [f"pc{d}" for d in range(8)] + ["xi", "xc", "sz", "de", "y", "hb"])

    def mamba_dir(d):
        p.dma("qs", "dtw", lambda e, s=dtwTbf[d]: e.dma_start(out=DTW[0:64, :], in_=s),
              reads=(f"dtwTbf{d}",), writes=("DTW",))
        p.dma("qs", "xpt", lambda e, s=xpTbf[d]: e.dma_start(out=XPT, in_=s),
              reads=(f"xpTbf{d}",), writes=("XPT",))

        for ch in range(NCH):
            if d == 0:
                t0 = ch * TC

                def xn_rhs(k, tb=t0):
                    return XN[:, k * 2048 + tb: k * 2048 + tb + TC]
            else:
                t_hi = 2047 - ch * TC

                def xn_rhs(k, th=t_hi):
                    return bass.AP(XN, k * 2048 + th, [[16384, 128], [-1, TC]])

            tailv = bass.AP(PARS, PA_TAIL, [[1664, 128], [3, 16], [1, 3]])
            if ch > 0:
                p.op(VEC, lambda e, tv=tailv: e.tensor_copy(
                    tv, ar3(O_XI, 259, 16, 3, inner_off=256)),
                    reads=("xi",), writes=("xtail",))
            for e2c in range(32):
                sl = WS_S[e2c % 2]
                p.dma("qs", f"wq{e2c % 2}", lambda e, ec=e2c, s=sl: e.dma_start(
                    out=WST[:, s:s + 1024], in_=inTbf[d][ec]),
                    reads=(f"inTbf{d}",), writes=(f"wq{e2c % 2}",))
                pi = (e2c // 4) % 2
                r = e2c % 4
                for k in range(8):
                    mm(psr(pi, r), WST[:, sl + k * 128: sl + (k + 1) * 128], xn_rhs(k),
                       k == 0, k == 7,
                       reads=(f"wq{e2c % 2}", "xn"), writes=prn(pi, r))
                if r == 3:
                    rd = prn(pi, 0, 4)
                    srcb = bass.AP(PB[pi], 0, [[2048, 128], [512, 4], [1, 256]])
                    if e2c < 16:
                        dst = ar3(O_XI + (e2c - 3) * 259, 259, 4, 256, inner_off=3)
                        p.op(SCA, lambda e, o=dst, i=srcb:
                             e.activation(o, i, AF.Copy, bias=0.0, scale=1.0),
                             reads=rd, writes=("xi",))
                    else:
                        dst = ar3(O_SZ + (e2c - 16 - 3) * 256, 256, 4, 256)
                        p.op(SCA, lambda e, o=dst, i=srcb:
                             e.activation(o, i, AF.Silu, bias=0.0, scale=1.0),
                             reads=rd, writes=("sz",))
            if ch > 0:
                p.op(VEC, lambda e, tv=tailv: e.tensor_copy(
                    ar3(O_XI, 259, 16, 3), tv), reads=("xtail",), writes=("xi",), strict=True)
            else:
                p.op(VEC, lambda e: e.memset(ar3(O_XI, 259, 16, 3), 0.0),
                     reads=(), writes=("xi",))
            if DBG and d == 0 and ch == 0:
                dump(0, ar3(O_XI, 259, 16, 256, inner_off=3), ("xi",))
                dump(1, ar(O_SZ, 4096), ("sz",))
                p.dma("qx", "dbg", lambda e: e.dma_start(out=dbg2, in_=XN[:]),
                      reads=("xn",), writes=())
                p.dma("qx", "dbg", lambda e: e.dma_start(
                    out=dbg[5].bitcast(BF16)[:, 0:1024], in_=inTbf[0][0]),
                    reads=("inTbf0",), writes=())
                p.dma("qx", "dbg", lambda e: e.dma_start(
                    out=dbg[6].bitcast(BF16)[:, 0:1024], in_=inTbf[0][1]),
                    reads=("inTbf0",), writes=())
                dump(7, bass.AP(PARS, 0, [[1664, 128], [1, 1664]]),
                     ("stat_m", "stat_r", "stat_s"), w=1664)
            for ec in range(ECT):
                xcv = ar(O_XC + ec * 256, 256)
                cw = PARS[:, PA_CONVW[d] + ec * 4: PA_CONVW[d] + ec * 4 + 4]
                cb = PARS[:, PA_CONVB[d] + ec: PA_CONVB[d] + ec + 1]
                p.op(VEC, lambda e, o=xcv, i=ar(O_XI + ec * 259, 256), w=cw[:, 0:1], b=cb:
                     e.tensor_scalar(o, i, w, b, OP.mult, OP.add),
                     reads=("xi", f"convw{d}", f"convb{d}"), writes=("xc",), strict=True)
                for k in (1, 2, 3):
                    p.op(VEC, lambda e, o=xcv, i=ar(O_XI + ec * 259 + k, 256), w=cw[:, k:k + 1]:
                         e.scalar_tensor_tensor(o, i, w, o, OP.mult, OP.add),
                         reads=("xi", f"convw{d}", "xc"), writes=("xc",))
            p.op(SCA, lambda e: e.activation(ar(O_XC, 4096), ar(O_XC, 4096), AF.Silu,
                                             bias=0.0, scale=1.0),
                 reads=("xc",), writes=("xc",))
            if DBG and d == 0 and ch == 0:
                dump(2, ar(O_XC, 4096), ("xc",))
            xcbf = ar(O_XB, 2048).bitcast(BF16)
            p.op(VEC, lambda e, o=xcbf: e.tensor_copy(o, ar(O_XC, 4096)),
                 reads=("xc",), writes=("xbf",))
            for ec in range(ECT):
                mm(PB[0][0:96, 0:256], XPT[:, ec * 96:(ec + 1) * 96],
                   xcbf[:, ec * 256:(ec + 1) * 256],
                   ec == 0, ec == 15, reads=("XPT", "xbf"), writes=prn(0, 0))
            p.op(SCA, lambda e: e.activation(LOW[0:96, LW_DBL:LW_DBL + 256], PB[0][0:96, 0:256],
                                             AF.Copy, bias=0.0, scale=1.0),
                 reads=prn(0, 0), writes=("dbl",))
            par = ch % 2
            p.dma("qs", f"bcr{par}", lambda e, t=bcr[par]: e.dma_start(
                out=t, in_=LOW[64:96, LW_DBL:LW_DBL + 256]),
                reads=("dbl",), writes=(f"bcr{par}",))
            dtbf = LOW[:, LW_DTBF:LW_DTBF + 128].bitcast(BF16)
            p.op(VEC, lambda e, o=dtbf[0:64, :]: e.tensor_copy(o, LOW[0:64, LW_DBL:LW_DBL + 256]),
                 reads=("dbl",), writes=("dtbf",))
            for ec in range(ECT):
                r = ec % 4
                mm(psr(1, r), DTW[0:64, ec * 128:(ec + 1) * 128], dtbf[0:64, :],
                   True, True, reads=("DTW", "dtbf"), writes=prn(1, r))
                p.op(SCA, lambda e, o=ar(O_DE + ec * 256, 256), i=psr(1, r),
                     b=PARS[:, PA_DTB[d] + ec:PA_DTB[d] + ec + 1]:
                     e.activation(o, i, AF.Exp, bias=b, scale=1.0),
                     reads=prn(1, r) + (f"dtb{d}",), writes=("de",))
            p.op(SCA, lambda e: e.activation(ar(O_DE, 4096), ar(O_DE, 4096), AF.Ln,
                                             bias=1.0, scale=1.0),
                 reads=("de",), writes=("de",))
            if INTERLEAVE:
                moe_units(2)
            if DBG and d == 0 and ch == 0:
                dump(3, bass.AP(LOW, 0, [[2048, 128], [0, 16], [1, 256]]), ("dbl",))
                dump(4, ar(O_DE, 4096), ("de",))
            for ec in range(ECT):
                p.op(VEC, lambda e, o=ar(O_Y + ec * 256, 256), i=ar(O_XC + ec * 256, 256),
                     s=PARS[:, PA_D[d] + ec:PA_D[d] + ec + 1]:
                     e.tensor_single_scalar(o, i, s, OP.mult),
                     reads=("xc", f"D{d}"), writes=("y",))
            p.op(VEC, lambda e: e.tensor_mul(ar(O_XC, 4096), ar(O_DE, 4096), ar(O_XC, 4096)),
                 reads=("de", "xc"), writes=("xc",))
            st = PARS[:, PA_ST[d]:PA_ST[d] + 256]
            for n in range(NST):
                bsl = n % 2
                bb = BCB[:, bsl * 512: bsl * 512 + 256]
                cc = BCB[:, bsl * 512 + 256: bsl * 512 + 512]
                p.dma("qs", f"bb{bsl}", lambda e, o=bb, t=bcr[par], r=n: e.dma_start(
                    out=o, in_=bass.AP(t.tensor, r * TC, [[0, 128], [1, TC]])),
                    reads=(f"bcr{par}",), writes=(f"bb{bsl}",))
                p.dma("qs", f"cb{bsl}", lambda e, o=cc, t=bcr[par], r=16 + n: e.dma_start(
                    out=o, in_=bass.AP(t.tensor, r * TC, [[0, 128], [1, TC]])),
                    reads=(f"bcr{par}",), writes=(f"cb{bsl}",))
                for eh in range(2):
                    dsl = (n * 2 + eh) % 2
                    hcb = ar(O_DBU + dsl * 2048, 2048)
                    p.op(POOL, lambda e, o=ar3(O_DBU + dsl * 2048, 256, 8, 256),
                         u0=ar3(O_XC + eh * 2048, 256, 8, 256),
                         b0=bass.AP(BCB, bsl * 512, [[1024, 128], [0, 8], [1, 256]]):
                         e.tensor_tensor(o, u0, b0, OP.mult),
                         reads=("xc", f"bb{bsl}"), writes=(f"dbu{dsl}",))
                    for ei in range(8):
                        ec = eh * 8 + ei
                        dasl = ec % 4
                        da = ar(O_DAS[dasl], 256)
                        p.op(SCA, lambda e, o=da, i=ar(O_DE + ec * 256, 256),
                             s=PARS[:, PA_A[d] + ec * 16 + n: PA_A[d] + ec * 16 + n + 1]:
                             e.activation(o, i, AF.Exp, bias=0.0, scale=s),
                             reads=("de", f"A{d}"), writes=(f"da{dasl}",))
                        hsl = ar(O_HB + (ec % 8) * 256, 256)
                        if ch == 0:
                            scrd = (f"da{dasl}", f"dbu{dsl}")
                            init = 0.0
                        else:
                            scrd = (f"da{dasl}", f"dbu{dsl}", "st")
                            init = st[:, ec * 16 + n: ec * 16 + n + 1]
                        p.op(VEC, lambda e, o=hsl, a=da,
                             b=ar(O_DBU + dsl * 2048 + ei * 256, 256), i0=init:
                             e.tensor_tensor_scan(o, a, b, i0, OP.mult, OP.add),
                             reads=scrd, writes=("hb",))
                        p.op(VEC, lambda e, o=hcb[:, ei * 256:(ei + 1) * 256], a=hsl, b=cc:
                             e.tensor_mul(o, a, b),
                             reads=("hb", f"cb{bsl}"), writes=(f"dbu{dsl}",))
                    if ch < NCH - 1:
                        cdst = bass.AP(PARS, PA_ST[d] + (eh * 8) * 16 + n, [[1664, 128], [16, 8]])
                        csrc = bass.AP(AR, O_HB + 255, [[AR_W, 128], [256, 8]])
                        p.op(SCA, lambda e, o=cdst, i=csrc:
                             e.activation(o, i, AF.Copy, bias=0.0, scale=1.0),
                             reads=("hb",), writes=("st",))
                    eng = POOL if n % 2 == 0 else VEC
                    p.op(eng, lambda e, o=ar(O_Y + eh * 2048, 2048), a=hcb:
                         e.tensor_tensor(o, a, o, OP.add),
                         reads=(f"dbu{dsl}", "y"), writes=("y",))
            if DBG and d == 0 and ch == 0:
                dump(5, ar(O_Y, 4096), ("y",))
                dump(6, ar(O_XC, 4096), ("xc",))
            ybf = ar(O_XB, 2048).bitcast(BF16)
            p.op(VEC, lambda e, o=ybf: e.tensor_tensor(o, ar(O_Y, 4096), ar(O_SZ, 4096), OP.mult),
                 reads=("y", "sz", "xbf"), writes=("xbf",))
            for d1c in range(8):
                sl = (d1c % 2) * 2048
                wn = (f"wq{(d1c % 2) * 2}", f"wq{(d1c % 2) * 2 + 1}")
                p.dma("qs", f"wo{d1c % 2}", lambda e, c=d1c, s=sl: e.dma_start(
                    out=WST[:, s:s + 2048], in_=outTbf[d][c]),
                    reads=(f"outTbf{d}",), writes=wn)
                pi, r = d1c // 4, d1c % 4
                for k in range(16):
                    mm(psr(pi, r), WST[:, sl + k * 128: sl + (k + 1) * 128],
                       ybf[:, k * 256:(k + 1) * 256],
                       k == 0, k == 15, reads=wn + ("xbf",),
                       writes=prn(pi, r))
            fbf3 = ar(O_FBF, 1024).bitcast(BF16).rearrange("q (a b) -> q a b", b=256)
            for pi in range(2):
                half = fbf3[:, pi * 4:(pi + 1) * 4, :]
                fdst = half if d == 0 else half[:, :, ::-1]
                srcb = bass.AP(PB[pi], 0, [[2048, 128], [512, 4], [1, 256]])
                p.op(SCA, lambda e, o=fdst, i=srcb:
                     e.activation(o, i, AF.Copy, bias=0.0, scale=1.0),
                     reads=prn(pi, 0, 4), writes=("fbf",))
            fflat = ar(O_FBF, 1024).bitcast(BF16)
            t_start = ch * TC if d == 0 else L - (ch + 1) * TC
            for k in range(8):
                sl = WS_S[k % 2]
                p.dma("qs", f"wq{k % 2}", lambda e, kk=k, s=sl: e.dma_start(
                    out=WST[:, s:s + 1024], in_=mergeTbf[d, kk]),
                    reads=("mergeTbf",), writes=(f"wq{k % 2}",))
                for ti in range(2):
                    for nh in range(2):
                        mm(PB[1][:, ti * 1024 + nh * 512: ti * 1024 + nh * 512 + 512],
                           fflat[:, k * 256 + ti * 128: k * 256 + ti * 128 + 128],
                           WST[:, sl + nh * 512: sl + (nh + 1) * 512],
                           k == 0, k == 7,
                           reads=(f"wq{k % 2}", "fbf"),
                           writes=prn(1, ti * 2 + nh))
            for ti in range(2):
                stg = ar(O_ST, 1024) if ti == 0 else ar(O_FBF, 1024)
                sw = (f"stg{ti}",) if ti == 0 else ("stg1", "fbf")
                p.op(SCA, lambda e, o=stg, i=PB[1][:, ti * 1024: ti * 1024 + 1024]:
                     e.activation(o, i, AF.Copy, bias=0.0, scale=1.0),
                     reads=prn(1, ti * 2, 2), writes=sw)
                p.dma("qx", f"at{d}", lambda e, s=stg, a=t_start + ti * 128:
                      e.dma_start(out=attn_d[d][a:a + 128, :], in_=s),
                      reads=sw, writes=(f"attn{d}",))
            if INTERLEAVE:
                moe_units(2)

    do_moe = pmask[3] >= 1
    GT = LOW[:, LW_B2:LW_B2 + 1024].bitcast(BF16)
    UNITS = [(j, tq) for j in range(NEXP) for tq in range(NCH)] if do_moe else []
    unit_pos = [0]

    def moe_prepass():
        for tt in range(16 if do_moe else 0):
            pt = PB[0][0:8, (tt % 4) * 512:(tt % 4) * 512 + 128]
            p.op(TEN, lambda e, i=gates[:, tt * 8:(tt + 1) * 8], o=pt:
                 e.transpose(o, i, IDE[:]),
                 reads=(f"g{tt}", "IDE"), writes=prn(0, tt % 4))
            p.op(SCA, lambda e, o=GT[0:8, tt * 128:(tt + 1) * 128], i=pt:
                 e.activation(o, i, AF.Copy, bias=0.0, scale=1.0),
                 reads=prn(0, tt % 4),
                 writes=("gT",) if tt else ("gT", "b2f"))
        for tt in range(16 if do_moe else 0):
            for nh in range(2):
                mm(PB[1][:, nh * 512:(nh + 1) * 512], GT[0:8, tt * 128:(tt + 1) * 128],
                   b2bfv[0:8, nh * 512:(nh + 1) * 512],
                   True, True, reads=("gT", "b2bf"), writes=prn(1, nh))
            ys = tt % 2
            stg = ar(O_YSB + ys * 1024, 1024)
            p.op(SCA, lambda e, o=stg, i=PB[1][:, 0:1024]:
                 e.activation(o, i, AF.Copy, bias=0.0, scale=1.0),
                 reads=prn(1, 0, 2), writes=(f"ysb{ys}",))
            p.dma("qx", "fs", lambda e, s=stg, a=tt: e.dma_start(
                out=ffn_d[a * 128:(a + 1) * 128, :], in_=s),
                reads=(f"ysb{ys}",), writes=(f"ffn{tt}",))

    def moe_units(k):
        for j, tq in UNITS[unit_pos[0]:unit_pos[0] + k]:
            unit_pos[0] += 1
            if tq == 0:
                extra = ("stat_m", "stat_r", "stat_s", "stat_e", "stat_v") if j == 0 else ()
                p.dma("qx", "b1", lambda e, jj=j: e.dma_start(
                    out=PARS[:, PA_STAT:PA_STAT + 32], in_=b1blk[jj]),
                    reads=(), writes=("b1",) + extra)
            for m in range(32):
                ws = m % 2
                p.dma("qx", f"mw1{ws}", lambda e, jj=j, mi=m, s=ws: e.dma_start(
                    out=MOEW[:, s * 1024:(s + 1) * 1024], in_=w1bf[jj, mi]),
                    reads=(f"w1bf{j}",), writes=(f"w1p{ws}",), pipe=True)
                p.dma("qs", f"mw2{ws}", lambda e, jj=j, mi=m, s=ws: e.dma_start(
                    out=MOEW[:, 2048 + s * 1024: 2048 + (s + 1) * 1024],
                    in_=w2bf[jj][:, mi * 1024:(mi + 1) * 1024]),
                    reads=(f"w2bf{j}",), writes=(f"w2p{ws}",), pipe=True)
                r = m % 4
                for k8 in range(8):
                    mm(psr(0, r), MOEW[:, ws * 1024 + k8 * 128: ws * 1024 + (k8 + 1) * 128],
                       XN[:, k8 * 2048 + tq * TC: k8 * 2048 + tq * TC + TC],
                       k8 == 0, k8 == 7, reads=(f"w1p{ws}", "xn"), writes=prn(0, r))
                p.op(SCA, lambda e, o=HP[:, ws * 256:(ws + 1) * 256], i=psr(0, r),
                     b=PARS[:, PA_STAT + m:PA_STAT + m + 1]:
                     e.activation(o, i, AF.Relu, bias=b, scale=1.0),
                     reads=prn(0, r) + ("b1",), writes=(f"hp{ws}",))
                for ti in range(2):
                    for nh in range(2):
                        mm(PB[1][:, ti * 1024 + nh * 512: ti * 1024 + nh * 512 + 512],
                           HP[:, ws * 256 + ti * 128: ws * 256 + ti * 128 + 128],
                           MOEW[:, 2048 + ws * 1024 + nh * 512:
                                2048 + ws * 1024 + (nh + 1) * 512],
                           m == 0, m == 31,
                           reads=(f"hp{ws}", f"w2p{ws}"), writes=prn(1, ti * 2 + nh))
            for ti in range(2):
                tt = tq * 2 + ti
                stg = ar(O_YSB + ti * 1024, 1024)
                gcol = gates[:, tt * 8 + j: tt * 8 + j + 1]
                p.op(SCA, lambda e, o=stg, i=PB[1][:, ti * 1024: ti * 1024 + 1024], g=gcol:
                     e.activation(o, i, AF.Copy, bias=0.0, scale=g),
                     reads=prn(1, ti * 2, 2) + (f"g{tt}",), writes=(f"ysb{ti}",))
                p.dma("qg", f"fa{ti}", lambda e, s=stg, a=tt: e.dma_start(
                    out=ffn_d[a * 128:(a + 1) * 128, :], in_=s, accum_op=OP.add),
                    reads=(f"ysb{ti}",) + (f"ffn{tt}",), writes=(f"ffn{tt}",))

    moe_prepass()
    if pmask[1]:
        mamba_dir(0)
    if pmask[2]:
        mamba_dir(1)
    moe_units(len(UNITS) - unit_pos[0])

    p.fence(VEC, ["xi", "xc", "sz", "de", "y", "hb", "dbu0", "dbu1",
                  "da0", "da1", "xbf", "fbf", "stg0", "stg1", "xtail", "DTW", "XPT",
                  "w1p0", "w1p1", "w2p0", "w2p1", "hp0", "hp1", "ysb0", "ysb1",
                  "gT", "b1", "b2bf", "xn", "fence4"])

    for tt in range(16 if pmask[4] else 0):
        pp = tt % 2
        xt = ar(0 + pp * 1024, 1024)
        af_ = ar(8192 + pp * 1024, 1024)
        ab_ = ar(16384 + pp * 1024, 1024)
        ff_ = ar(24576 + pp * 1024, 1024)
        a0 = tt * 128
        p.dma("qx", "p4x", lambda e, o=xt, a=a0: e.dma_start(out=o, in_=xb[a:a + 128, :]),
              reads=("fence4",), writes=(f"p4x{pp}",))
        p.dma("qx", "p4a", lambda e, o=af_, a=a0: e.dma_start(out=o, in_=attn_d[0][a:a + 128, :]),
              reads=("attn0", "fence4"), writes=(f"p4a{pp}",))
        p.dma("qx", "p4b", lambda e, o=ab_, a=a0: e.dma_start(out=o, in_=attn_d[1][a:a + 128, :]),
              reads=("attn1", "fence4"), writes=(f"p4b{pp}",))
        p.dma("qx", "p4f", lambda e, o=ff_, a=a0: e.dma_start(out=o, in_=ffn_d[a:a + 128, :]),
              reads=(f"ffn{tt}", "fence4"), writes=(f"p4f{pp}",))
        p.op(VEC, lambda e, a=xt, b=af_: e.tensor_add(a, a, b),
             reads=(f"p4x{pp}", f"p4a{pp}"), writes=(f"p4x{pp}",))
        p.op(VEC, lambda e, a=xt, b=ab_: e.tensor_add(a, a, b),
             reads=(f"p4x{pp}", f"p4b{pp}"), writes=(f"p4x{pp}",))
        p.op(VEC, lambda e, a=xt, b=ff_: e.tensor_add(a, a, b),
             reads=(f"p4x{pp}", f"p4f{pp}"), writes=(f"p4x{pp}",))
        p.dma("qx", "out", lambda e, s=xt, a=a0: e.dma_start(out=out_h[a:a + 128, :], in_=s),
              reads=(f"p4x{pp}",), writes=("outd",))

    if pmask[4]:
        p.op(SCA, lambda e: e.nop(), reads=("outd",), writes=())

    p.emit(nc, sems, lambda nm: e_(nc.semaphore(nm)))
    stack.close()
    return nc


def prep_shared(inputs):

    shared = {}
    for d, pre in ((0, "fwd"), (1, "bwd")):
        c = "fb"[d]
        ip = np.asarray(inputs[f"L0_attn_{pre}_in_proj"], dtype=np.float32)
        shared[f"inT_{c}"] = np.ascontiguousarray(
            ip.T.reshape(8, 128, 32, 128).transpose(2, 1, 0, 3).reshape(32, 128, 1024))
        op_ = np.asarray(inputs[f"L0_attn_{pre}_out_proj"], dtype=np.float32)
        shared[f"outT_{c}"] = np.ascontiguousarray(
            op_.T.reshape(16, 128, 8, 128).transpose(2, 1, 0, 3).reshape(8, 128, 2048))
        xp = np.asarray(inputs[f"L0_attn_{pre}_x_proj"], dtype=np.float32)
        shared[f"xpT_{c}"] = np.ascontiguousarray(
            xp.T.reshape(16, 128, 96).transpose(1, 0, 2).reshape(128, 1536))
        shared[f"dtwT_{c}"] = np.ascontiguousarray(
            np.asarray(inputs[f"L0_attn_{pre}_dt_w"], dtype=np.float32).T)
        shared[f"convw_{c}"] = np.ascontiguousarray(
            np.asarray(inputs[f"L0_attn_{pre}_conv_w"], dtype=np.float32)
            .reshape(16, 128, 4).transpose(1, 0, 2).reshape(128, 64))
        shared[f"convb_{c}"] = np.ascontiguousarray(
            np.asarray(inputs[f"L0_attn_{pre}_conv_b"], dtype=np.float32).reshape(16, 128).T)
        shared[f"dtb_{c}"] = np.ascontiguousarray(
            np.asarray(inputs[f"L0_attn_{pre}_dt_b"], dtype=np.float32).reshape(16, 128).T)
        shared[f"D_{c}"] = np.ascontiguousarray(
            np.asarray(inputs[f"L0_attn_{pre}_D"], dtype=np.float32).reshape(16, 128).T)
        shared[f"A_{c}"] = np.ascontiguousarray(
            np.asarray(inputs[f"L0_attn_{pre}_A_log"], dtype=np.float32)
            .reshape(16, 128, 16).transpose(1, 0, 2).reshape(128, 256))
    mw = np.asarray(inputs["L0_attn_merge_w"], dtype=np.float32)
    shared["mergeT2"] = np.ascontiguousarray(mw.T.reshape(2, 8, 128, 1024))
    shared["rw"] = np.ascontiguousarray(
        np.asarray(inputs["L0_ffn_router_w"], dtype=np.float32)
        .reshape(8, 128, 8).transpose(1, 0, 2).reshape(128, 64))
    shared["rb"] = np.ascontiguousarray(
        np.asarray(inputs["L0_ffn_router_b"], dtype=np.float32).reshape(1, 8))
    w1 = np.asarray(inputs["L0_ffn_w1"], dtype=np.float32)
    shared["w1p"] = np.ascontiguousarray(
        w1.reshape(8, 8, 128, 32, 128).transpose(0, 3, 2, 1, 4).reshape(8, 32, 128, 1024))
    w2 = np.asarray(inputs["L0_ffn_w2"], dtype=np.float32)
    shared["w2blk"] = np.ascontiguousarray(
        w2.reshape(8, 32, 128, 1024).transpose(0, 2, 1, 3).reshape(8, 128, 32768))
    shared["b1blk"] = np.ascontiguousarray(
        np.asarray(inputs["L0_ffn_b1"], dtype=np.float32).reshape(8, 32, 128).transpose(0, 2, 1))
    shared["b2"] = np.ascontiguousarray(np.asarray(inputs["L0_ffn_b2"], dtype=np.float32))
    shared["ident"] = np.eye(128, dtype=np.float32)
    return shared


def kernel(**inputs):
    x = np.ascontiguousarray(np.asarray(inputs["L0_attn_x"], dtype=np.float32))
    shared = prep_shared(inputs)
    nc = build_nc()
    in_maps = []
    for b in range(8):
        m = dict(shared)
        m["xb"] = np.ascontiguousarray(x[b])
        in_maps.append(m)
    res = run_bass_kernel_spmd(nc, in_maps, core_ids=list(range(8)))
    out = np.stack([np.asarray(r["out"]) for r in res.results], axis=0)
    return out.astype(np.float32)


if __name__ == "__main__":
    nc = build_nc()
    print("build ok:", nc.n_instructions(), "instructions")


# version 08
# speedup vs baseline: 1.4738x; 1.4738x over previous
import numpy as np
import ml_dtypes
import concourse.bass as bass
import concourse.mybir as mybir
from concourse.bass_utils import run_bass_kernel_spmd

F32 = mybir.dt.float32
BF16 = mybir.dt.bfloat16
AF = mybir.ActivationFunctionType
OP = mybir.AluOpType
AX = mybir.AxisListType

L = 2048
DM = 1024
E2 = 2048
NST = 16
TC = 256
NCH = L // TC
ECT = 16
NEXP = 8
DBG = False
AR_W = 35648
INTERLEAVE = True


class Prog:
    ENGINES = ("tensor", "vector", "scalar", "sync", "gpsimd")
    QUEUES = {"qs": "sync", "qx": "scalar", "qg": "gpsimd"}

    def __init__(self):
        self.ops = {e: [] for e in self.ENGINES}
        self.lanes = {}
        self.buf = {}

    def _deps(self, reads, writes):
        deps = []
        for b in reads:
            st = self.buf.get(b)
            if st and st[0] is not None:
                deps.append(st[0])
        for b in writes:
            st = self.buf.get(b)
            if st:
                if st[0] is not None:
                    deps.append(st[0])
                deps.extend(st[1])
        return deps

    def _update(self, h, reads, writes):
        for b in writes:
            self.buf[b] = [h, []]
        for b in reads:
            st = self.buf.setdefault(b, [None, []])
            st[1].append(h)

    def op(self, eng, fn, reads=(), writes=(), deps=(), strict=False):
        dl = list(deps) + self._deps(reads, writes)
        h = ("e", eng, len(self.ops[eng]))
        self.ops[eng].append({"fn": fn, "deps": dl, "q": None, "strict": strict})
        self._update(h, reads, writes)
        return h

    def dma(self, queue, lane, fn, reads=(), writes=(), deps=(), pipe=False):
        dl = list(deps) + self._deps(reads, writes)
        n = self.lanes.get(lane, 0) + 1
        self.lanes[lane] = n
        h = ("q", lane, n)
        if n > 1 and not pipe:
            dl.append(("q", lane, n - 1))
        self.ops[self.QUEUES[queue]].append({"fn": fn, "deps": dl, "q": lane,
                                             "strict": False})
        self._update(h, reads, writes)
        return h

    def fence(self, eng, names):
        return self.op(eng, lambda e: e.engine_nop(), reads=(), writes=tuple(names))

    def emit(self, nc, sems, mksem):
        qsems = {ln: mksem("q_" + ln) for ln in self.lanes}
        flagged = {e: set() for e in self.ENGINES}
        for eng in self.ENGINES:
            for o in self.ops[eng]:
                for d in o["deps"]:
                    if d[0] == "e" and (d[1] != eng or o["strict"]):
                        flagged[d[1]].add(d[2])
        semval = {}
        for eng in self.ENGINES:
            c = 0
            vals = []
            for i in range(len(self.ops[eng])):
                if i in flagged[eng]:
                    c += 1
                vals.append(c)
            semval[eng] = vals

        def make_body(eng):
            ops = self.ops[eng]
            fl = flagged[eng]

            def body(e):
                hw = {}
                for i, o in enumerate(ops):
                    for d in o["deps"]:
                        if d[0] == "e":
                            if d[1] == eng and not o["strict"]:
                                continue
                            key, val, sem = d[1], semval[d[1]][d[2]], sems[d[1]]
                        else:
                            key, val, sem = d[1], 16 * d[2], qsems[d[1]]
                        if hw.get(key, -1) < val:
                            e.wait_ge(sem, val)
                            hw[key] = val
                    instr = o["fn"](e)
                    if o["q"] is not None:
                        instr.then_inc(qsems[o["q"]], 16)
                    elif i in fl:
                        instr.then_inc(sems[eng], 1)

            return body

        with nc.Block() as block:
            block.tensor(make_body("tensor"))
            block.vector(make_body("vector"))
            block.scalar(make_body("scalar"))
            block.sync(make_body("sync"))
            block.gpsimd(make_body("gpsimd"))


def build_nc(pmask=(1, 1, 1, 4, 1)):
    nc = bass.Bass("TRN2", target_bir_lowering=False, detect_race_conditions=False)
    p = Prog()

    def ein(name, shape, dt=F32):
        return nc.dram_tensor(name, list(shape), dt, kind="ExternalInput").ap()

    xb = ein("xb", (L, DM))
    inT = {0: ein("inT_f", (32, 128, 1024)), 1: ein("inT_b", (32, 128, 1024))}
    outT = {0: ein("outT_f", (8, 128, 2048)), 1: ein("outT_b", (8, 128, 2048))}
    mergeT2 = ein("mergeT2", (2, 8, 128, 1024))
    xpT = {0: ein("xpT_f", (128, 1536)), 1: ein("xpT_b", (128, 1536))}
    dtwT = {0: ein("dtwT_f", (64, 2048)), 1: ein("dtwT_b", (64, 2048))}
    convw = {0: ein("convw_f", (128, 64)), 1: ein("convw_b", (128, 64))}
    convb = {0: ein("convb_f", (128, 16)), 1: ein("convb_b", (128, 16))}
    dtb = {0: ein("dtb_f", (128, 16)), 1: ein("dtb_b", (128, 16))}
    Dparm = {0: ein("D_f", (128, 16)), 1: ein("D_b", (128, 16))}
    Alog = {0: ein("A_f", (128, 256)), 1: ein("A_b", (128, 256))}
    rw = ein("rw", (128, 64))
    rb = ein("rb", (1, 8))
    w1p = ein("w1p", (NEXP, 32, 128, 1024))
    w2blk = ein("w2blk", (NEXP, 128, 32768))
    b1blk = ein("b1blk", (NEXP, 128, 32))
    b2 = ein("b2", (NEXP, 1024))
    ident = ein("ident", (128, 128))
    out_h = nc.dram_tensor("out", [L, DM], F32, kind="ExternalOutput").ap()

    def idr(name, shape, dt):
        kind = "ExternalOutput" if (DBG and name in ("attn_f", "attn_b", "ffn")) else "Internal"
        return nc.dram_tensor(name, list(shape), dt, kind=kind).ap()

    inTbf = {0: idr("inTbf_f", (32, 128, 1024), BF16), 1: idr("inTbf_b", (32, 128, 1024), BF16)}
    outTbf = {0: idr("outTbf_f", (8, 128, 2048), BF16), 1: idr("outTbf_b", (8, 128, 2048), BF16)}
    mergeTbf = idr("mergeTbf", (2, 8, 128, 1024), BF16)
    xpTbf = {0: idr("xpTbf_f", (128, 1536), BF16), 1: idr("xpTbf_b", (128, 1536), BF16)}
    dtwTbf = {0: idr("dtwTbf_f", (64, 2048), BF16), 1: idr("dtwTbf_b", (64, 2048), BF16)}
    w1bf = idr("w1bf", (NEXP, 32, 128, 1024), BF16)
    w2bf = idr("w2bf", (NEXP, 128, 32768), BF16)
    attn_d = {0: idr("attn_f", (L, DM), F32), 1: idr("attn_b", (L, DM), F32)}
    ffn_d = idr("ffn", (L, DM), F32)
    bcr = [idr("bcr0", (32, TC), F32), idr("bcr1", (32, TC), F32)]
    dbg = nc.dram_tensor("dbg", [8, 128, 4096], F32,
                         kind="ExternalOutput" if DBG else "Internal").ap()
    dbg2 = nc.dram_tensor("dbg2", [128, 16384], BF16,
                          kind="ExternalOutput" if DBG else "Internal").ap()

    def dump(slot, src_ap, rd, w=4096):
        if DBG:
            p.dma("qx", "dbg", lambda e, s=src_ap, t=dbg[slot][:, 0:w]: e.dma_start(out=t, in_=s),
                  reads=rd, writes=())

    from contextlib import ExitStack
    stack = ExitStack()
    e_ = stack.enter_context
    AR = e_(nc.sbuf_tensor("AR", [128, AR_W], F32))
    XN = e_(nc.sbuf_tensor("XN", [128, 16384], BF16))
    PARS = e_(nc.sbuf_tensor("PARS", [128, 1664], F32))
    LOW = e_(nc.sbuf_tensor("LOW", [128, 2048], F32))
    IDE = e_(nc.sbuf_tensor("IDE", [128, 128], F32))
    WST = e_(nc.sbuf_tensor("WST", [128, 4096], BF16))
    BCB = e_(nc.sbuf_tensor("BCB", [128, 1024], F32))
    RWS = e_(nc.sbuf_tensor("RWS", [128, 64], F32))
    MOEW = e_(nc.sbuf_tensor("MOEW", [128, 4096], BF16))
    HP = e_(nc.sbuf_tensor("HP", [128, 512], BF16))
    PB = [e_(nc.psum_tensor("PB0", [128, 2048], F32)),
          e_(nc.psum_tensor("PB1", [128, 2048], F32))]

    sems = {en: e_(nc.semaphore("s_" + en)) for en in Prog.ENGINES}

    def ar(off, w):
        return AR[:, off:off + w]

    DTW = AR[:, 31296:32320].bitcast(BF16)
    XPT = AR[:, 32320:33088].bitcast(BF16)

    def ar3(off, outer_step, outer_cnt, inner_cnt, inner_step=1, inner_off=0):
        return bass.AP(AR, off + inner_off,
                       [[AR_W, 128], [outer_step, outer_cnt], [inner_step, inner_cnt]])

    def psr(i, r, w=256):
        return PB[i][:, r * 512: r * 512 + w]

    def prn(i, r, n=1):
        return tuple(f"p{i}r{r + k}" for k in range(n))

    O_XI, O_XC, O_SZ, O_DE, O_Y = 0, 4160, 8256, 12352, 16448
    O_HB, O_DBU = 20544, 22592
    O_DAS = (26688, 26944, 35136, 35392)
    O_XB = 27200
    O_FBF = 29248
    O_ST = 30272
    O_YSB = 33088
    P0_XT, P0_PC, P0_SC = 0, 16384, 17408
    PA_CONVW = {0: 0, 1: 64}
    PA_CONVB = {0: 128, 1: 144}
    PA_DTB = {0: 160, 1: 176}
    PA_D = {0: 192, 1: 208}
    PA_A = {0: 224, 1: 480}
    PA_ST = {0: 736, 1: 992}
    PA_STAT = 1248
    PA_LG = 1344
    PA_G = 1472
    PA_RB = 1600
    PA_TAIL = 1612
    LW_DBL = 0
    LW_B2 = 256
    LW_B2BF = 1280
    LW_DTBF = 1792
    LW_SC = 1920
    WS_S = [0, 1024]

    TEN, VEC, SCA, POOL = "tensor", "vector", "scalar", "gpsimd"

    def mm(out_ap, lhsT, rhs, start, stop, reads, writes, deps=()):
        return p.op(TEN, lambda e, a=out_ap, b=lhsT, c=rhs, s1=start, s2=stop:
                    e.matmul(a, b, c, start=s1, stop=s2),
                    reads=reads, writes=writes, deps=deps)

    def act(out, in_, func, bias, scale, reads, writes, accum=None, deps=(), strict=False):
        return p.op(SCA, lambda e, o=out, i=in_, f=func, b=bias, s=scale, au=accum:
                    e.activation(o, i, f, bias=b, scale=s, accum_out=au),
                    reads=reads, writes=writes, deps=deps, strict=strict)

    def flat2(ap):
        n = 1
        for s in ap.shape:
            n *= s
        assert n % 128 == 0
        return bass.AP(ap.tensor, 0, [[n // 128, 128], [1, n // 128]])

    for d in (0, 1):
        p.dma("qg", "cast0", lambda e, s=flat2(inT[d]), t=flat2(inTbf[d]): e.dma_start(out=t, in_=s),
              writes=(f"inTbf{d}",))
        p.dma("qg", "cast0", lambda e, s=flat2(outT[d]), t=flat2(outTbf[d]): e.dma_start(out=t, in_=s),
              writes=(f"outTbf{d}",))
        p.dma("qg", "cast0", lambda e, s=flat2(xpT[d]), t=flat2(xpTbf[d]): e.dma_start(out=t, in_=s),
              writes=(f"xpTbf{d}",))
        p.dma("qg", "cast0", lambda e, s=flat2(dtwT[d]), t=flat2(dtwTbf[d]): e.dma_start(out=t, in_=s),
              writes=(f"dtwTbf{d}",))
    p.dma("qg", "cast0", lambda e, s=flat2(mergeT2), t=flat2(mergeTbf): e.dma_start(out=t, in_=s),
          writes=("mergeTbf",))

    def flat2o(ap_tensor, off, n):
        return bass.AP(ap_tensor, off, [[n // 128, 128], [1, n // 128]])

    NW1 = 32 * 128 * 1024
    NW2 = 128 * 32768
    for j in range(NEXP):
        p.dma("qg", "cast1", lambda e, jj=j: e.dma_start(
            out=flat2o(w1bf.tensor, jj * NW1, NW1), in_=flat2o(w1p.tensor, jj * NW1, NW1)),
            writes=(f"w1bf{j}",))
        p.dma("qg", "cast1", lambda e, jj=j: e.dma_start(
            out=flat2o(w2bf.tensor, jj * NW2, NW2), in_=flat2o(w2blk.tensor, jj * NW2, NW2)),
            writes=(f"w2bf{j}",))

    p.dma("qx", "par", lambda e: e.dma_start(out=IDE[:], in_=ident), writes=("IDE",))
    for d in (0, 1):
        for src, off, wdt, nm in ((convw[d], PA_CONVW[d], 64, f"convw{d}"),
                                  (convb[d], PA_CONVB[d], 16, f"convb{d}"),
                                  (dtb[d], PA_DTB[d], 16, f"dtb{d}"),
                                  (Dparm[d], PA_D[d], 16, f"D{d}"),
                                  (Alog[d], PA_A[d], 256, f"A{d}")):
            p.dma("qx", "par", lambda e, s=src, o=off, w=wdt: e.dma_start(
                out=PARS[:, o:o + w], in_=s), writes=(nm,))
    p.dma("qx", "par", lambda e: e.dma_start(out=LOW[0:8, LW_B2:LW_B2 + 1024], in_=b2),
          writes=("b2f",))
    p.dma("qx", "par", lambda e: e.dma_start(
        out=PARS[:, PA_RB:PA_RB + 8], in_=bass.AP(rb.tensor, 0, [[0, 128], [1, 8]])),
        writes=("rbb",))
    p.dma("qx", "par", lambda e: e.dma_start(out=RWS[:], in_=rw), writes=("rw",))

    for d in (0, 1):
        Aap = PARS[:, PA_A[d]:PA_A[d] + 256]
        act(Aap, Aap, AF.Exp, 0.0, 1.0, reads=(f"A{d}",), writes=(f"A{d}",))
        p.op(VEC, lambda e, a=Aap: e.tensor_scalar_mul(a, a, -1.0),
             reads=(f"A{d}",), writes=(f"A{d}",))
    b2bfv = LOW[:, LW_B2BF:LW_B2BF + 512].bitcast(BF16)
    p.op(VEC, lambda e: e.tensor_copy(b2bfv[0:8, :], LOW[0:8, LW_B2:LW_B2 + 1024]),
         reads=("b2f",), writes=("b2bf",))

    if not pmask[0]:
        p.op(SCA, lambda e: e.nop(), reads=(), writes=())
        p.emit(nc, sems, qsems)
        stack.close()
        return nc
    stat = PARS[:, PA_STAT:PA_STAT + 96]
    for tt in range(16):
        xt = ar(P0_XT + tt * 1024, 1024)
        p.dma("qx", "xt", lambda e, t=xt, a=tt: e.dma_start(out=t, in_=xb[a * 128:(a + 1) * 128, :]),
              writes=(f"xt{tt}",))
        act(ar(P0_SC, 1024), xt, AF.Copy, 0.0, 1.0, reads=(f"xt{tt}",),
            writes=("p0sc", "stat_s"), accum=stat[:, tt:tt + 1])
        act(ar(P0_SC, 1024), xt, AF.Square, 0.0, 1.0, reads=(f"xt{tt}",),
            writes=("p0sc", "stat_s"), accum=stat[:, 16 + tt:17 + tt])
    p.op(VEC, lambda e: e.tensor_scalar_mul(stat[:, 32:48], stat[:, 0:16], 1.0 / DM),
         reads=("stat_s",), writes=("stat_m",), strict=True)
    p.op(VEC, lambda e: e.tensor_scalar_mul(stat[:, 48:64], stat[:, 16:32], 1.0 / DM),
         reads=("stat_s",), writes=("stat_e",), strict=True)
    act(stat[:, 64:80], stat[:, 32:48], AF.Square, 0.0, 1.0,
        reads=("stat_m",), writes=("stat_v",))
    p.op(VEC, lambda e: e.tensor_sub(stat[:, 64:80], stat[:, 48:64], stat[:, 64:80]),
         reads=("stat_e", "stat_v"), writes=("stat_v",), strict=True)
    p.op(VEC, lambda e: e.tensor_scalar_add(stat[:, 64:80], stat[:, 64:80], 1e-5),
         reads=("stat_v",), writes=("stat_v",), strict=True)
    act(stat[:, 64:80], stat[:, 64:80], AF.Sqrt, 0.0, 1.0, reads=("stat_v",), writes=("stat_v",))
    p.op(VEC, lambda e: e.reciprocal(stat[:, 80:96], stat[:, 64:80]),
         reads=("stat_v",), writes=("stat_r",), strict=True)

    logits = PARS[:, PA_LG:PA_LG + 128]
    gates = PARS[:, PA_G:PA_G + 128]
    for tt in range(16):
        xt = ar(P0_XT + tt * 1024, 1024)
        xnf = ar(P0_SC, 1024)
        p.op(VEC, lambda e, t=xt, o=xnf, c=tt: e.tensor_scalar(
            o, t, stat[:, 32 + c:33 + c], stat[:, 80 + c:81 + c], OP.subtract, OP.mult),
            reads=(f"xt{tt}", "stat_m", "stat_r"), writes=("p0sc",), strict=True)
        for dk in range(8):
            r = dk % 4
            pt = psr(0, r, 128)
            p.op(TEN, lambda e, o=pt, i=xnf[:, dk * 128:(dk + 1) * 128]:
                 e.transpose(o, i, IDE[:]),
                 reads=("p0sc", "IDE"), writes=prn(0, r))
            p.op(SCA, lambda e, o=XN[:, dk * 2048 + tt * 128: dk * 2048 + (tt + 1) * 128], i=pt:
                 e.activation(o, i, AF.Copy, bias=0.0, scale=1.0),
                 reads=prn(0, r), writes=("xn",))
            p.op(SCA, lambda e, o=ar(P0_PC + dk * 128, 128), i=pt:
                 e.activation(o, i, AF.Copy, bias=0.0, scale=1.0),
                 reads=prn(0, r), writes=(f"pc{dk}",))
        for dk in range(8):
            mm(PB[1][:, 0:8], ar(P0_PC + dk * 128, 128), RWS[:, dk * 8:(dk + 1) * 8],
               dk == 0, dk == 7, reads=(f"pc{dk}", "rw"), writes=prn(1, 0))
        p.op(VEC, lambda e, o=logits[:, tt * 8:(tt + 1) * 8]: e.tensor_add(
            o, PB[1][:, 0:8], PARS[:, PA_RB:PA_RB + 8]),
            reads=prn(1, 0) + ("rbb",), writes=(f"lg{tt}",), strict=True)
    sc = LOW[:, LW_SC:LW_SC + 128]
    for tt in range(16):
        lg = logits[:, tt * 8:(tt + 1) * 8]
        gt = gates[:, tt * 8:(tt + 1) * 8]
        m1, m2, ex, dn, iv, g2 = (sc[:, i:i + 1] for i in range(6))
        l2, mk1, mk2, t1 = sc[:, 8:16], sc[:, 16:24], sc[:, 24:32], sc[:, 32:40]
        p.op(VEC, lambda e, a=m1, b=lg: e.reduce_max(a, b, axis=AX.X),
             reads=(f"lg{tt}",), writes=("t2",), strict=True)
        p.op(VEC, lambda e, a=mk1, b=lg, c=m1: e.tensor_single_scalar(a, b, c, OP.is_ge),
             reads=(f"lg{tt}", "t2"), writes=("t2",), strict=True)
        p.op(VEC, lambda e, a=l2, b=mk1, c=lg: e.scalar_tensor_tensor(
            a, b, -1e30, c, OP.mult, OP.add), reads=(f"lg{tt}", "t2"), writes=("t2",), strict=True)
        p.op(VEC, lambda e, a=m2, b=l2: e.reduce_max(a, b, axis=AX.X),
             reads=("t2",), writes=("t2",), strict=True)
        p.op(VEC, lambda e, a=mk2, b=l2, c=m2: e.tensor_single_scalar(a, b, c, OP.is_ge),
             reads=("t2",), writes=("t2",), strict=True)
        p.op(VEC, lambda e, a=ex, b=m2, c=m1: e.tensor_sub(a, b, c),
             reads=("t2",), writes=("t2",), strict=True)
        act(ex, ex, AF.Exp, 0.0, 1.0, reads=("t2",), writes=("t2",), strict=True)
        p.op(VEC, lambda e, a=dn, b=ex: e.tensor_scalar_add(a, b, 1.0),
             reads=("t2",), writes=("t2",), strict=True)
        p.op(VEC, lambda e, a=iv, b=dn: e.reciprocal(a, b), reads=("t2",), writes=("t2",), strict=True)
        p.op(VEC, lambda e, a=g2, b=ex, c=iv: e.tensor_mul(a, b, c),
             reads=("t2",), writes=("t2",), strict=True)
        p.op(VEC, lambda e, a=t1, b=mk1, c=iv: e.tensor_single_scalar(a, b, c, OP.mult),
             reads=("t2",), writes=("t2",), strict=True)
        p.op(VEC, lambda e, a=gt, b=mk2, c=g2, d0=t1: e.scalar_tensor_tensor(
            a, b, c, d0, OP.mult, OP.add), reads=("t2",), writes=(f"g{tt}",), strict=True)

    p.fence(VEC, [f"xt{t}" for t in range(16)] + ["p0sc"] +
            [f"pc{d}" for d in range(8)] + ["xi", "xc", "sz", "de", "y", "hb"])

    def mamba_dir(d):
        p.dma("qs", "dtw", lambda e, s=dtwTbf[d]: e.dma_start(out=DTW[0:64, :], in_=s),
              reads=(f"dtwTbf{d}",), writes=("DTW",))
        p.dma("qs", "xpt", lambda e, s=xpTbf[d]: e.dma_start(out=XPT, in_=s),
              reads=(f"xpTbf{d}",), writes=("XPT",))

        for ch in range(NCH):
            if d == 0:
                t0 = ch * TC

                def xn_rhs(k, tb=t0):
                    return XN[:, k * 2048 + tb: k * 2048 + tb + TC]
            else:
                t_hi = 2047 - ch * TC

                def xn_rhs(k, th=t_hi):
                    return bass.AP(XN, k * 2048 + th, [[16384, 128], [-1, TC]])

            tailv = bass.AP(PARS, PA_TAIL, [[1664, 128], [3, 16], [1, 3]])
            if ch > 0:
                p.op(VEC, lambda e, tv=tailv: e.tensor_copy(
                    tv, ar3(O_XI, 259, 16, 3, inner_off=256)),
                    reads=("xi",), writes=("xtail",))
            for e2c in range(32):
                sl = WS_S[e2c % 2]
                p.dma("qs", f"wq{e2c % 2}", lambda e, ec=e2c, s=sl: e.dma_start(
                    out=WST[:, s:s + 1024], in_=inTbf[d][ec]),
                    reads=(f"inTbf{d}",), writes=(f"wq{e2c % 2}",))
                pi = 0
                r = e2c % 4
                for k in range(8):
                    mm(psr(pi, r), WST[:, sl + k * 128: sl + (k + 1) * 128], xn_rhs(k),
                       k == 0, k == 7,
                       reads=(f"wq{e2c % 2}", "xn"), writes=prn(pi, r))
                if r == 3:
                    rd = prn(pi, 0, 4)
                    srcb = bass.AP(PB[pi], 0, [[2048, 128], [512, 4], [1, 256]])
                    if e2c < 16:
                        dst = ar3(O_XI + (e2c - 3) * 259, 259, 4, 256, inner_off=3)
                        p.op(SCA, lambda e, o=dst, i=srcb:
                             e.activation(o, i, AF.Copy, bias=0.0, scale=1.0),
                             reads=rd, writes=("xi",))
                    else:
                        dst = ar3(O_SZ + (e2c - 16 - 3) * 256, 256, 4, 256)
                        p.op(SCA, lambda e, o=dst, i=srcb:
                             e.activation(o, i, AF.Silu, bias=0.0, scale=1.0),
                             reads=rd, writes=("sz",))
            if ch > 0:
                p.op(VEC, lambda e, tv=tailv: e.tensor_copy(
                    ar3(O_XI, 259, 16, 3), tv), reads=("xtail",), writes=("xi",), strict=True)
            else:
                p.op(VEC, lambda e: e.memset(ar3(O_XI, 259, 16, 3), 0.0),
                     reads=(), writes=("xi",))
            if DBG and d == 0 and ch == 0:
                dump(0, ar3(O_XI, 259, 16, 256, inner_off=3), ("xi",))
                dump(1, ar(O_SZ, 4096), ("sz",))
                p.dma("qx", "dbg", lambda e: e.dma_start(out=dbg2, in_=XN[:]),
                      reads=("xn",), writes=())
                p.dma("qx", "dbg", lambda e: e.dma_start(
                    out=dbg[5].bitcast(BF16)[:, 0:1024], in_=inTbf[0][0]),
                    reads=("inTbf0",), writes=())
                p.dma("qx", "dbg", lambda e: e.dma_start(
                    out=dbg[6].bitcast(BF16)[:, 0:1024], in_=inTbf[0][1]),
                    reads=("inTbf0",), writes=())
                dump(7, bass.AP(PARS, 0, [[1664, 128], [1, 1664]]),
                     ("stat_m", "stat_r", "stat_s"), w=1664)
            for ec in range(ECT):
                xcv = ar(O_XC + ec * 256, 256)
                cw = PARS[:, PA_CONVW[d] + ec * 4: PA_CONVW[d] + ec * 4 + 4]
                cb = PARS[:, PA_CONVB[d] + ec: PA_CONVB[d] + ec + 1]
                p.op(VEC, lambda e, o=xcv, i=ar(O_XI + ec * 259, 256), w=cw[:, 0:1], b=cb:
                     e.tensor_scalar(o, i, w, b, OP.mult, OP.add),
                     reads=("xi", f"convw{d}", f"convb{d}"), writes=("xc",), strict=True)
                for k in (1, 2, 3):
                    p.op(VEC, lambda e, o=xcv, i=ar(O_XI + ec * 259 + k, 256), w=cw[:, k:k + 1]:
                         e.scalar_tensor_tensor(o, i, w, o, OP.mult, OP.add),
                         reads=("xi", f"convw{d}", "xc"), writes=("xc",))
            p.op(SCA, lambda e: e.activation(ar(O_XC, 4096), ar(O_XC, 4096), AF.Silu,
                                             bias=0.0, scale=1.0),
                 reads=("xc",), writes=("xc",))
            if DBG and d == 0 and ch == 0:
                dump(2, ar(O_XC, 4096), ("xc",))
            xcbf = ar(O_XB, 2048).bitcast(BF16)
            p.op(VEC, lambda e, o=xcbf: e.tensor_copy(o, ar(O_XC, 4096)),
                 reads=("xc",), writes=("xbf",))
            for ec in range(ECT):
                mm(PB[0][0:96, 0:256], XPT[:, ec * 96:(ec + 1) * 96],
                   xcbf[:, ec * 256:(ec + 1) * 256],
                   ec == 0, ec == 15, reads=("XPT", "xbf"), writes=prn(0, 0))
            p.op(SCA, lambda e: e.activation(LOW[0:96, LW_DBL:LW_DBL + 256], PB[0][0:96, 0:256],
                                             AF.Copy, bias=0.0, scale=1.0),
                 reads=prn(0, 0), writes=("dbl",))
            par = ch % 2
            p.dma("qs", f"bcr{par}", lambda e, t=bcr[par]: e.dma_start(
                out=t, in_=LOW[64:96, LW_DBL:LW_DBL + 256]),
                reads=("dbl",), writes=(f"bcr{par}",))
            dtbf = LOW[:, LW_DTBF:LW_DTBF + 128].bitcast(BF16)
            p.op(VEC, lambda e, o=dtbf[0:64, :]: e.tensor_copy(o, LOW[0:64, LW_DBL:LW_DBL + 256]),
                 reads=("dbl",), writes=("dtbf",))
            for ec in range(ECT):
                r = ec % 4
                mm(psr(0, r), DTW[0:64, ec * 128:(ec + 1) * 128], dtbf[0:64, :],
                   True, True, reads=("DTW", "dtbf"), writes=prn(0, r))
                p.op(SCA, lambda e, o=ar(O_DE + ec * 256, 256), i=psr(0, r),
                     b=PARS[:, PA_DTB[d] + ec:PA_DTB[d] + ec + 1]:
                     e.activation(o, i, AF.Exp, bias=b, scale=1.0),
                     reads=prn(0, r) + (f"dtb{d}",), writes=("de",))
            p.op(SCA, lambda e: e.activation(ar(O_DE, 4096), ar(O_DE, 4096), AF.Ln,
                                             bias=1.0, scale=1.0),
                 reads=("de",), writes=("de",))
            if DBG and d == 0 and ch == 0:
                dump(3, bass.AP(LOW, 0, [[2048, 128], [0, 16], [1, 256]]), ("dbl",))
                dump(4, ar(O_DE, 4096), ("de",))
            for ec in range(ECT):
                p.op(VEC, lambda e, o=ar(O_Y + ec * 256, 256), i=ar(O_XC + ec * 256, 256),
                     s=PARS[:, PA_D[d] + ec:PA_D[d] + ec + 1]:
                     e.tensor_single_scalar(o, i, s, OP.mult),
                     reads=("xc", f"D{d}"), writes=("y",))
            p.op(VEC, lambda e: e.tensor_mul(ar(O_XC, 4096), ar(O_DE, 4096), ar(O_XC, 4096)),
                 reads=("de", "xc"), writes=("xc",))
            st = PARS[:, PA_ST[d]:PA_ST[d] + 256]
            for n in range(NST):
                bsl = n % 2
                bb = BCB[:, bsl * 512: bsl * 512 + 256]
                cc = BCB[:, bsl * 512 + 256: bsl * 512 + 512]
                p.dma("qs", f"bb{bsl}", lambda e, o=bb, t=bcr[par], r=n: e.dma_start(
                    out=o, in_=bass.AP(t.tensor, r * TC, [[0, 128], [1, TC]])),
                    reads=(f"bcr{par}",), writes=(f"bb{bsl}",))
                p.dma("qs", f"cb{bsl}", lambda e, o=cc, t=bcr[par], r=16 + n: e.dma_start(
                    out=o, in_=bass.AP(t.tensor, r * TC, [[0, 128], [1, TC]])),
                    reads=(f"bcr{par}",), writes=(f"cb{bsl}",))
                for eh in range(2):
                    dsl = (n * 2 + eh) % 2
                    hcb = ar(O_DBU + dsl * 2048, 2048)
                    p.op(POOL, lambda e, o=ar3(O_DBU + dsl * 2048, 256, 8, 256),
                         u0=ar3(O_XC + eh * 2048, 256, 8, 256),
                         b0=bass.AP(BCB, bsl * 512, [[1024, 128], [0, 8], [1, 256]]):
                         e.tensor_tensor(o, u0, b0, OP.mult),
                         reads=("xc", f"bb{bsl}"), writes=(f"dbu{dsl}",))
                    for ei in range(8):
                        ec = eh * 8 + ei
                        dasl = ec % 4
                        da = ar(O_DAS[dasl], 256)
                        p.op(SCA, lambda e, o=da, i=ar(O_DE + ec * 256, 256),
                             s=PARS[:, PA_A[d] + ec * 16 + n: PA_A[d] + ec * 16 + n + 1]:
                             e.activation(o, i, AF.Exp, bias=0.0, scale=s),
                             reads=("de", f"A{d}"), writes=(f"da{dasl}",))
                        hsl = ar(O_HB + (ec % 8) * 256, 256)
                        if ch == 0:
                            scrd = (f"da{dasl}", f"dbu{dsl}")
                            init = 0.0
                        else:
                            scrd = (f"da{dasl}", f"dbu{dsl}", "st")
                            init = st[:, ec * 16 + n: ec * 16 + n + 1]
                        p.op(VEC, lambda e, o=hsl, a=da,
                             b=ar(O_DBU + dsl * 2048 + ei * 256, 256), i0=init:
                             e.tensor_tensor_scan(o, a, b, i0, OP.mult, OP.add),
                             reads=scrd, writes=("hb",))
                        p.op(VEC, lambda e, o=hcb[:, ei * 256:(ei + 1) * 256], a=hsl, b=cc:
                             e.tensor_mul(o, a, b),
                             reads=("hb", f"cb{bsl}"), writes=(f"dbu{dsl}",))
                    if ch < NCH - 1:
                        cdst = bass.AP(PARS, PA_ST[d] + (eh * 8) * 16 + n, [[1664, 128], [16, 8]])
                        csrc = bass.AP(AR, O_HB + 255, [[AR_W, 128], [256, 8]])
                        p.op(SCA, lambda e, o=cdst, i=csrc:
                             e.activation(o, i, AF.Copy, bias=0.0, scale=1.0),
                             reads=("hb",), writes=("st",))
                    eng = POOL if n % 2 == 0 else VEC
                    p.op(eng, lambda e, o=ar(O_Y + eh * 2048, 2048), a=hcb:
                         e.tensor_tensor(o, a, o, OP.add),
                         reads=(f"dbu{dsl}", "y"), writes=("y",))
                    if INTERLEAVE:
                        moe_feed(4)
            if DBG and d == 0 and ch == 0:
                dump(5, ar(O_Y, 4096), ("y",))
                dump(6, ar(O_XC, 4096), ("xc",))
            ybf = ar(O_XB, 2048).bitcast(BF16)
            p.op(VEC, lambda e, o=ybf: e.tensor_tensor(o, ar(O_Y, 4096), ar(O_SZ, 4096), OP.mult),
                 reads=("y", "sz", "xbf"), writes=("xbf",))
            fbf3 = ar(O_FBF, 1024).bitcast(BF16).rearrange("q (a b) -> q a b", b=256)
            for d1c in range(8):
                sl = (d1c % 2) * 2048
                wn = (f"wq{(d1c % 2) * 2}", f"wq{(d1c % 2) * 2 + 1}")
                p.dma("qs", f"wo{d1c % 2}", lambda e, c=d1c, s=sl: e.dma_start(
                    out=WST[:, s:s + 2048], in_=outTbf[d][c]),
                    reads=(f"outTbf{d}",), writes=wn)
                r = d1c % 4
                for k in range(16):
                    mm(psr(0, r), WST[:, sl + k * 128: sl + (k + 1) * 128],
                       ybf[:, k * 256:(k + 1) * 256],
                       k == 0, k == 15, reads=wn + ("xbf",),
                       writes=prn(0, r))
                if r == 3:
                    wv = d1c // 4
                    half = fbf3[:, wv * 4:(wv + 1) * 4, :]
                    fdst = half if d == 0 else half[:, :, ::-1]
                    srcb = bass.AP(PB[0], 0, [[2048, 128], [512, 4], [1, 256]])
                    p.op(SCA, lambda e, o=fdst, i=srcb:
                         e.activation(o, i, AF.Copy, bias=0.0, scale=1.0),
                         reads=prn(0, 0, 4), writes=("fbf",))
            fflat = ar(O_FBF, 1024).bitcast(BF16)
            t_start = ch * TC if d == 0 else L - (ch + 1) * TC
            for k in range(8):
                sl = WS_S[k % 2]
                p.dma("qs", f"wq{k % 2}", lambda e, kk=k, s=sl: e.dma_start(
                    out=WST[:, s:s + 1024], in_=mergeTbf[d, kk]),
                    reads=("mergeTbf",), writes=(f"wq{k % 2}",))
                for ti in range(2):
                    for nh in range(2):
                        mm(PB[0][:, ti * 1024 + nh * 512: ti * 1024 + nh * 512 + 512],
                           fflat[:, k * 256 + ti * 128: k * 256 + ti * 128 + 128],
                           WST[:, sl + nh * 512: sl + (nh + 1) * 512],
                           k == 0, k == 7,
                           reads=(f"wq{k % 2}", "fbf"),
                           writes=prn(0, ti * 2 + nh))
            for ti in range(2):
                stg = ar(O_ST, 1024) if ti == 0 else ar(O_FBF, 1024)
                sw = (f"stg{ti}",) if ti == 0 else ("stg1", "fbf")
                p.op(SCA, lambda e, o=stg, i=PB[0][:, ti * 1024: ti * 1024 + 1024]:
                     e.activation(o, i, AF.Copy, bias=0.0, scale=1.0),
                     reads=prn(0, ti * 2, 2), writes=sw)
                p.dma("qx", f"at{d}", lambda e, s=stg, a=t_start + ti * 128:
                      e.dma_start(out=attn_d[d][a:a + 128, :], in_=s),
                      reads=sw, writes=(f"attn{d}",))

    do_moe = pmask[3] >= 1
    GT = LOW[:, LW_B2:LW_B2 + 1024].bitcast(BF16)
    UNITS = [(j, tq) for j in range(NEXP) for tq in range(NCH)] if do_moe else []
    unit_pos = [0]

    def moe_prepass():
        for tt in range(16 if do_moe else 0):
            pt = PB[0][0:8, (tt % 4) * 512:(tt % 4) * 512 + 128]
            p.op(TEN, lambda e, i=gates[:, tt * 8:(tt + 1) * 8], o=pt:
                 e.transpose(o, i, IDE[:]),
                 reads=(f"g{tt}", "IDE"), writes=prn(0, tt % 4))
            p.op(SCA, lambda e, o=GT[0:8, tt * 128:(tt + 1) * 128], i=pt:
                 e.activation(o, i, AF.Copy, bias=0.0, scale=1.0),
                 reads=prn(0, tt % 4),
                 writes=("gT",) if tt else ("gT", "b2f"))
        for tt in range(16 if do_moe else 0):
            for nh in range(2):
                mm(PB[1][:, nh * 512:(nh + 1) * 512], GT[0:8, tt * 128:(tt + 1) * 128],
                   b2bfv[0:8, nh * 512:(nh + 1) * 512],
                   True, True, reads=("gT", "b2bf"), writes=prn(1, nh))
            ys = tt % 2
            stg = ar(O_YSB + ys * 1024, 1024)
            p.op(SCA, lambda e, o=stg, i=PB[1][:, 0:1024]:
                 e.activation(o, i, AF.Copy, bias=0.0, scale=1.0),
                 reads=prn(1, 0, 2), writes=(f"ysb{ys}",))
            p.dma("qx", "fs", lambda e, s=stg, a=tt: e.dma_start(
                out=ffn_d[a * 128:(a + 1) * 128, :], in_=s),
                reads=(f"ysb{ys}",), writes=(f"ffn{tt}",))

    STEPS = [(j, tq, m) for j, tq in UNITS for m in range(32)]

    def w1load(g):
        if g >= len(STEPS):
            return
        j, tq, m = STEPS[g]
        ws = g % 2
        p.dma("qx", f"mw1{ws}", lambda e, jj=j, mi=m, s=ws: e.dma_start(
            out=MOEW[:, s * 1024:(s + 1) * 1024], in_=w1bf[jj, mi]),
            reads=(f"w1bf{j}",), writes=(f"w1p{ws}",), pipe=True)

    def w2load(g):
        j, tq, m = STEPS[g]
        ws = g % 2
        p.dma("qs", f"mw2{ws}", lambda e, jj=j, mi=m, s=ws: e.dma_start(
            out=MOEW[:, 2048 + s * 1024: 2048 + (s + 1) * 1024],
            in_=w2bf[jj][:, mi * 1024:(mi + 1) * 1024]),
            reads=(f"w2bf{j}",), writes=(f"w2p{ws}",), pipe=True)

    def emit_A(g):
        j, tq, m = STEPS[g]
        ws = g % 2
        r = m % 4
        for k8 in range(8):
            mm(psr(0, r), MOEW[:, ws * 1024 + k8 * 128: ws * 1024 + (k8 + 1) * 128],
               XN[:, k8 * 2048 + tq * TC: k8 * 2048 + tq * TC + TC],
               k8 == 0, k8 == 7, reads=(f"w1p{ws}", "xn"), writes=prn(0, r))

    def emit_relu(g):
        j, tq, m = STEPS[g]
        ws = g % 2
        r = m % 4
        p.op(SCA, lambda e, o=HP[:, ws * 256:(ws + 1) * 256], i=psr(0, r),
             b=PARS[:, PA_STAT + m:PA_STAT + m + 1]:
             e.activation(o, i, AF.Relu, bias=b, scale=1.0),
             reads=prn(0, r) + ("b1",), writes=(f"hp{ws}",))

    def emit_B(g):
        j, tq, m = STEPS[g]
        ws = g % 2
        for ti in range(2):
            for nh in range(2):
                mm(PB[1][:, ti * 1024 + nh * 512: ti * 1024 + nh * 512 + 512],
                   HP[:, ws * 256 + ti * 128: ws * 256 + ti * 128 + 128],
                   MOEW[:, 2048 + ws * 1024 + nh * 512:
                        2048 + ws * 1024 + (nh + 1) * 512],
                   m == 0, m == 31,
                   reads=(f"hp{ws}", f"w2p{ws}"), writes=prn(1, ti * 2 + nh))
        if m == 31:
            for ti in range(2):
                tt = tq * 2 + ti
                stg = ar(O_YSB + ti * 1024, 1024)
                gcol = gates[:, tt * 8 + j: tt * 8 + j + 1]
                p.op(SCA, lambda e, o=stg, i=PB[1][:, ti * 1024: ti * 1024 + 1024],
                     g=gcol: e.activation(o, i, AF.Copy, bias=0.0, scale=g),
                     reads=prn(1, ti * 2, 2) + (f"g{tt}",), writes=(f"ysb{ti}",))
                p.dma("qg", f"fa{ti}", lambda e, s=stg, a=tt: e.dma_start(
                    out=ffn_d[a * 128:(a + 1) * 128, :], in_=s, accum_op=OP.add),
                    reads=(f"ysb{ti}",) + (f"ffn{tt}",), writes=(f"ffn{tt}",))

    def moe_gen():
        for g in range(len(STEPS)):
            j, tq, m = STEPS[g]
            if tq == 0 and m == 0:
                extra = ("stat_m", "stat_r", "stat_s", "stat_e", "stat_v") if j == 0 else ()
                p.dma("qx", "b1", lambda e, jj=j: e.dma_start(
                    out=PARS[:, PA_STAT:PA_STAT + 32], in_=b1blk[jj]),
                    reads=(), writes=("b1",) + extra)
            if g == 0:
                w1load(0)
            w1load(g + 1)
            if g >= 2:
                emit_B(g - 2)
            w2load(g)
            emit_A(g)
            if g >= 1:
                emit_relu(g - 1)
            yield
        if STEPS:
            emit_relu(len(STEPS) - 1)
            emit_B(len(STEPS) - 2)
            emit_B(len(STEPS) - 1)

    MOE_IT = moe_gen()

    def moe_feed(k):
        for _ in range(k):
            if next(MOE_IT, "done") == "done":
                return

    moe_prepass()
    if pmask[1]:
        mamba_dir(0)
    if pmask[2]:
        mamba_dir(1)
    moe_feed(10 ** 9)

    p.fence(VEC, ["xi", "xc", "sz", "de", "y", "hb", "dbu0", "dbu1",
                  "da0", "da1", "xbf", "fbf", "stg0", "stg1", "xtail", "DTW", "XPT",
                  "w1p0", "w1p1", "w2p0", "w2p1", "hp0", "hp1", "ysb0", "ysb1",
                  "gT", "b1", "b2bf", "xn", "fence4"])

    for tt in range(16 if pmask[4] else 0):
        pp = tt % 2
        xt = ar(0 + pp * 1024, 1024)
        af_ = ar(8192 + pp * 1024, 1024)
        ab_ = ar(16384 + pp * 1024, 1024)
        ff_ = ar(24576 + pp * 1024, 1024)
        a0 = tt * 128
        p.dma("qx", "p4x", lambda e, o=xt, a=a0: e.dma_start(out=o, in_=xb[a:a + 128, :]),
              reads=("fence4",), writes=(f"p4x{pp}",))
        p.dma("qx", "p4a", lambda e, o=af_, a=a0: e.dma_start(out=o, in_=attn_d[0][a:a + 128, :]),
              reads=("attn0", "fence4"), writes=(f"p4a{pp}",))
        p.dma("qx", "p4b", lambda e, o=ab_, a=a0: e.dma_start(out=o, in_=attn_d[1][a:a + 128, :]),
              reads=("attn1", "fence4"), writes=(f"p4b{pp}",))
        p.dma("qx", "p4f", lambda e, o=ff_, a=a0: e.dma_start(out=o, in_=ffn_d[a:a + 128, :]),
              reads=(f"ffn{tt}", "fence4"), writes=(f"p4f{pp}",))
        p.op(VEC, lambda e, a=xt, b=af_: e.tensor_add(a, a, b),
             reads=(f"p4x{pp}", f"p4a{pp}"), writes=(f"p4x{pp}",))
        p.op(VEC, lambda e, a=xt, b=ab_: e.tensor_add(a, a, b),
             reads=(f"p4x{pp}", f"p4b{pp}"), writes=(f"p4x{pp}",))
        p.op(VEC, lambda e, a=xt, b=ff_: e.tensor_add(a, a, b),
             reads=(f"p4x{pp}", f"p4f{pp}"), writes=(f"p4x{pp}",))
        p.dma("qx", "out", lambda e, s=xt, a=a0: e.dma_start(out=out_h[a:a + 128, :], in_=s),
              reads=(f"p4x{pp}",), writes=("outd",))

    if pmask[4]:
        p.op(SCA, lambda e: e.nop(), reads=("outd",), writes=())

    p.emit(nc, sems, lambda nm: e_(nc.semaphore(nm)))
    stack.close()
    return nc


def prep_shared(inputs):

    shared = {}
    for d, pre in ((0, "fwd"), (1, "bwd")):
        c = "fb"[d]
        ip = np.asarray(inputs[f"L0_attn_{pre}_in_proj"], dtype=np.float32)
        shared[f"inT_{c}"] = np.ascontiguousarray(
            ip.T.reshape(8, 128, 32, 128).transpose(2, 1, 0, 3).reshape(32, 128, 1024))
        op_ = np.asarray(inputs[f"L0_attn_{pre}_out_proj"], dtype=np.float32)
        shared[f"outT_{c}"] = np.ascontiguousarray(
            op_.T.reshape(16, 128, 8, 128).transpose(2, 1, 0, 3).reshape(8, 128, 2048))
        xp = np.asarray(inputs[f"L0_attn_{pre}_x_proj"], dtype=np.float32)
        shared[f"xpT_{c}"] = np.ascontiguousarray(
            xp.T.reshape(16, 128, 96).transpose(1, 0, 2).reshape(128, 1536))
        shared[f"dtwT_{c}"] = np.ascontiguousarray(
            np.asarray(inputs[f"L0_attn_{pre}_dt_w"], dtype=np.float32).T)
        shared[f"convw_{c}"] = np.ascontiguousarray(
            np.asarray(inputs[f"L0_attn_{pre}_conv_w"], dtype=np.float32)
            .reshape(16, 128, 4).transpose(1, 0, 2).reshape(128, 64))
        shared[f"convb_{c}"] = np.ascontiguousarray(
            np.asarray(inputs[f"L0_attn_{pre}_conv_b"], dtype=np.float32).reshape(16, 128).T)
        shared[f"dtb_{c}"] = np.ascontiguousarray(
            np.asarray(inputs[f"L0_attn_{pre}_dt_b"], dtype=np.float32).reshape(16, 128).T)
        shared[f"D_{c}"] = np.ascontiguousarray(
            np.asarray(inputs[f"L0_attn_{pre}_D"], dtype=np.float32).reshape(16, 128).T)
        shared[f"A_{c}"] = np.ascontiguousarray(
            np.asarray(inputs[f"L0_attn_{pre}_A_log"], dtype=np.float32)
            .reshape(16, 128, 16).transpose(1, 0, 2).reshape(128, 256))
    mw = np.asarray(inputs["L0_attn_merge_w"], dtype=np.float32)
    shared["mergeT2"] = np.ascontiguousarray(mw.T.reshape(2, 8, 128, 1024))
    shared["rw"] = np.ascontiguousarray(
        np.asarray(inputs["L0_ffn_router_w"], dtype=np.float32)
        .reshape(8, 128, 8).transpose(1, 0, 2).reshape(128, 64))
    shared["rb"] = np.ascontiguousarray(
        np.asarray(inputs["L0_ffn_router_b"], dtype=np.float32).reshape(1, 8))
    w1 = np.asarray(inputs["L0_ffn_w1"], dtype=np.float32)
    shared["w1p"] = np.ascontiguousarray(
        w1.reshape(8, 8, 128, 32, 128).transpose(0, 3, 2, 1, 4).reshape(8, 32, 128, 1024))
    w2 = np.asarray(inputs["L0_ffn_w2"], dtype=np.float32)
    shared["w2blk"] = np.ascontiguousarray(
        w2.reshape(8, 32, 128, 1024).transpose(0, 2, 1, 3).reshape(8, 128, 32768))
    shared["b1blk"] = np.ascontiguousarray(
        np.asarray(inputs["L0_ffn_b1"], dtype=np.float32).reshape(8, 32, 128).transpose(0, 2, 1))
    shared["b2"] = np.ascontiguousarray(np.asarray(inputs["L0_ffn_b2"], dtype=np.float32))
    shared["ident"] = np.eye(128, dtype=np.float32)
    return shared


def kernel(**inputs):
    x = np.ascontiguousarray(np.asarray(inputs["L0_attn_x"], dtype=np.float32))
    shared = prep_shared(inputs)
    nc = build_nc()
    in_maps = []
    for b in range(8):
        m = dict(shared)
        m["xb"] = np.ascontiguousarray(x[b])
        in_maps.append(m)
    res = run_bass_kernel_spmd(nc, in_maps, core_ids=list(range(8)))
    out = np.stack([np.asarray(r["out"]) for r in res.results], axis=0)
    return out.astype(np.float32)


if __name__ == "__main__":
    nc = build_nc()
    print("build ok:", nc.n_instructions(), "instructions")


# version 09
# speedup vs baseline: 1.5130x; 1.0266x over previous
import numpy as np
import ml_dtypes
import concourse.bass as bass
import concourse.mybir as mybir
from concourse.bass_utils import run_bass_kernel_spmd

F32 = mybir.dt.float32
BF16 = mybir.dt.bfloat16
AF = mybir.ActivationFunctionType
OP = mybir.AluOpType
AX = mybir.AxisListType

L = 2048
DM = 1024
E2 = 2048
NST = 16
TC = 256
NCH = L // TC
ECT = 16
NEXP = 8
DBG = False
AR_W = 35648
INTERLEAVE = True


class Prog:
    ENGINES = ("tensor", "vector", "scalar", "sync", "gpsimd")
    QUEUES = {"qs": "sync", "qx": "scalar", "qg": "gpsimd"}

    def __init__(self):
        self.ops = {e: [] for e in self.ENGINES}
        self.lanes = {}
        self.buf = {}

    def _deps(self, reads, writes):
        deps = []
        for b in reads:
            st = self.buf.get(b)
            if st and st[0] is not None:
                deps.append(st[0])
        for b in writes:
            st = self.buf.get(b)
            if st:
                if st[0] is not None:
                    deps.append(st[0])
                deps.extend(st[1])
        return deps

    def _update(self, h, reads, writes):
        for b in writes:
            self.buf[b] = [h, []]
        for b in reads:
            st = self.buf.setdefault(b, [None, []])
            st[1].append(h)

    def op(self, eng, fn, reads=(), writes=(), deps=(), strict=False):
        dl = list(deps) + self._deps(reads, writes)
        h = ("e", eng, len(self.ops[eng]))
        self.ops[eng].append({"fn": fn, "deps": dl, "q": None, "strict": strict})
        self._update(h, reads, writes)
        return h

    def dma(self, queue, lane, fn, reads=(), writes=(), deps=(), pipe=False):
        dl = list(deps) + self._deps(reads, writes)
        n = self.lanes.get(lane, 0) + 1
        self.lanes[lane] = n
        h = ("q", lane, n)
        if n > 1 and not pipe:
            dl.append(("q", lane, n - 1))
        self.ops[self.QUEUES[queue]].append({"fn": fn, "deps": dl, "q": lane,
                                             "strict": False})
        self._update(h, reads, writes)
        return h

    def fence(self, eng, names):
        return self.op(eng, lambda e: e.engine_nop(), reads=(), writes=tuple(names))

    def emit(self, nc, sems, mksem):
        qsems = {ln: mksem("q_" + ln) for ln in self.lanes}
        flagged = {e: set() for e in self.ENGINES}
        for eng in self.ENGINES:
            for o in self.ops[eng]:
                for d in o["deps"]:
                    if d[0] == "e" and (d[1] != eng or o["strict"]):
                        flagged[d[1]].add(d[2])
        semval = {}
        for eng in self.ENGINES:
            c = 0
            vals = []
            for i in range(len(self.ops[eng])):
                if i in flagged[eng]:
                    c += 1
                vals.append(c)
            semval[eng] = vals

        def make_body(eng):
            ops = self.ops[eng]
            fl = flagged[eng]

            def body(e):
                hw = {}
                for i, o in enumerate(ops):
                    for d in o["deps"]:
                        if d[0] == "e":
                            if d[1] == eng and not o["strict"]:
                                continue
                            key, val, sem = d[1], semval[d[1]][d[2]], sems[d[1]]
                        else:
                            key, val, sem = d[1], 16 * d[2], qsems[d[1]]
                        if hw.get(key, -1) < val:
                            e.wait_ge(sem, val)
                            hw[key] = val
                    instr = o["fn"](e)
                    if o["q"] is not None:
                        instr.then_inc(qsems[o["q"]], 16)
                    elif i in fl:
                        instr.then_inc(sems[eng], 1)

            return body

        with nc.Block() as block:
            block.tensor(make_body("tensor"))
            block.vector(make_body("vector"))
            block.scalar(make_body("scalar"))
            block.sync(make_body("sync"))
            block.gpsimd(make_body("gpsimd"))


def build_nc(pmask=(1, 1, 1, 4, 1)):
    nc = bass.Bass("TRN2", target_bir_lowering=False, detect_race_conditions=False)
    p = Prog()

    def ein(name, shape, dt=F32):
        return nc.dram_tensor(name, list(shape), dt, kind="ExternalInput").ap()

    xb = ein("xb", (L, DM))
    inT = {0: ein("inT_f", (32, 128, 1024)), 1: ein("inT_b", (32, 128, 1024))}
    outT = {0: ein("outT_f", (8, 128, 2048)), 1: ein("outT_b", (8, 128, 2048))}
    mergeT2 = ein("mergeT2", (2, 8, 128, 1024))
    xpT = {0: ein("xpT_f", (128, 1536)), 1: ein("xpT_b", (128, 1536))}
    dtwT = {0: ein("dtwT_f", (64, 2048)), 1: ein("dtwT_b", (64, 2048))}
    convw = {0: ein("convw_f", (128, 64)), 1: ein("convw_b", (128, 64))}
    convb = {0: ein("convb_f", (128, 16)), 1: ein("convb_b", (128, 16))}
    dtb = {0: ein("dtb_f", (128, 16)), 1: ein("dtb_b", (128, 16))}
    Dparm = {0: ein("D_f", (128, 16)), 1: ein("D_b", (128, 16))}
    Alog = {0: ein("A_f", (128, 256)), 1: ein("A_b", (128, 256))}
    rw = ein("rw", (128, 64))
    rb = ein("rb", (1, 8))
    w1p = ein("w1p", (NEXP, 32, 128, 1024))
    w2blk = ein("w2blk", (NEXP, 128, 32768))
    b1blk = ein("b1blk", (NEXP, 128, 32))
    b2 = ein("b2", (NEXP, 1024))
    ident = ein("ident", (128, 128))
    out_h = nc.dram_tensor("out", [L, DM], F32, kind="ExternalOutput").ap()

    def idr(name, shape, dt):
        kind = "ExternalOutput" if (DBG and name in ("attn_f", "attn_b", "ffn")) else "Internal"
        return nc.dram_tensor(name, list(shape), dt, kind=kind).ap()

    inTbf = {0: idr("inTbf_f", (32, 128, 1024), BF16), 1: idr("inTbf_b", (32, 128, 1024), BF16)}
    outTbf = {0: idr("outTbf_f", (8, 128, 2048), BF16), 1: idr("outTbf_b", (8, 128, 2048), BF16)}
    mergeTbf = idr("mergeTbf", (2, 8, 128, 1024), BF16)
    xpTbf = {0: idr("xpTbf_f", (128, 1536), BF16), 1: idr("xpTbf_b", (128, 1536), BF16)}
    dtwTbf = {0: idr("dtwTbf_f", (64, 2048), BF16), 1: idr("dtwTbf_b", (64, 2048), BF16)}
    w1bf = idr("w1bf", (NEXP, 32, 128, 1024), BF16)
    w2bf = idr("w2bf", (NEXP, 128, 32768), BF16)
    attn_d = {0: idr("attn_f", (L, DM), F32), 1: idr("attn_b", (L, DM), F32)}
    ffn_d = idr("ffn", (L, DM), F32)
    bcr = [idr("bcr0", (32, TC), F32), idr("bcr1", (32, TC), F32)]
    dbg = nc.dram_tensor("dbg", [8, 128, 4096], F32,
                         kind="ExternalOutput" if DBG else "Internal").ap()
    dbg2 = nc.dram_tensor("dbg2", [128, 16384], BF16,
                          kind="ExternalOutput" if DBG else "Internal").ap()

    def dump(slot, src_ap, rd, w=4096):
        if DBG:
            p.dma("qx", "dbg", lambda e, s=src_ap, t=dbg[slot][:, 0:w]: e.dma_start(out=t, in_=s),
                  reads=rd, writes=())

    from contextlib import ExitStack
    stack = ExitStack()
    e_ = stack.enter_context
    AR = e_(nc.sbuf_tensor("AR", [128, AR_W], F32))
    XN = e_(nc.sbuf_tensor("XN", [128, 16384], BF16))
    PARS = e_(nc.sbuf_tensor("PARS", [128, 1664], F32))
    LOW = e_(nc.sbuf_tensor("LOW", [128, 2048], F32))
    IDE = e_(nc.sbuf_tensor("IDE", [128, 128], F32))
    WST = e_(nc.sbuf_tensor("WST", [128, 4096], BF16))
    BCB = e_(nc.sbuf_tensor("BCB", [128, 1024], F32))
    RWS = e_(nc.sbuf_tensor("RWS", [128, 64], F32))
    MOEW = e_(nc.sbuf_tensor("MOEW", [128, 4096], BF16))
    HP = e_(nc.sbuf_tensor("HP", [128, 512], BF16))
    PB = [e_(nc.psum_tensor("PB0", [128, 2048], F32)),
          e_(nc.psum_tensor("PB1", [128, 2048], F32))]

    sems = {en: e_(nc.semaphore("s_" + en)) for en in Prog.ENGINES}

    def ar(off, w):
        return AR[:, off:off + w]

    DTW = AR[:, 31296:32320].bitcast(BF16)
    XPT = AR[:, 32320:33088].bitcast(BF16)

    def ar3(off, outer_step, outer_cnt, inner_cnt, inner_step=1, inner_off=0):
        return bass.AP(AR, off + inner_off,
                       [[AR_W, 128], [outer_step, outer_cnt], [inner_step, inner_cnt]])

    def psr(i, r, w=256):
        return PB[i][:, r * 512: r * 512 + w]

    def prn(i, r, n=1):
        return tuple(f"p{i}r{r + k}" for k in range(n))

    O_XI, O_XC, O_SZ, O_DE, O_Y = 0, 4160, 8256, 12352, 16448
    O_HB, O_DBU = 20544, 22592
    O_DAS = (26688, 26944, 35136, 35392)
    O_XB = 27200
    O_FBF = 29248
    O_ST = 30272
    O_YSB = 33088
    P0_XT, P0_PC, P0_SC = 0, 16384, 17408
    PA_CONVW = {0: 0, 1: 64}
    PA_CONVB = {0: 128, 1: 144}
    PA_DTB = {0: 160, 1: 176}
    PA_D = {0: 192, 1: 208}
    PA_A = {0: 224, 1: 480}
    PA_ST = {0: 736, 1: 992}
    PA_STAT = 1248
    PA_LG = 1344
    PA_G = 1472
    PA_RB = 1600
    PA_TAIL = 1612
    LW_DBL = 0
    LW_B2 = 256
    LW_B2BF = 1280
    LW_DTBF = 1792
    LW_SC = 1920
    WS_S = [0, 1024]

    TEN, VEC, SCA, POOL = "tensor", "vector", "scalar", "gpsimd"

    def mm(out_ap, lhsT, rhs, start, stop, reads, writes, deps=()):
        return p.op(TEN, lambda e, a=out_ap, b=lhsT, c=rhs, s1=start, s2=stop:
                    e.matmul(a, b, c, start=s1, stop=s2),
                    reads=reads, writes=writes, deps=deps)

    def act(out, in_, func, bias, scale, reads, writes, accum=None, deps=(), strict=False):
        return p.op(SCA, lambda e, o=out, i=in_, f=func, b=bias, s=scale, au=accum:
                    e.activation(o, i, f, bias=b, scale=s, accum_out=au),
                    reads=reads, writes=writes, deps=deps, strict=strict)

    def flat2(ap):
        n = 1
        for s in ap.shape:
            n *= s
        assert n % 128 == 0
        return bass.AP(ap.tensor, 0, [[n // 128, 128], [1, n // 128]])

    for d in (0, 1):
        p.dma("qg", "cast0", lambda e, s=flat2(inT[d]), t=flat2(inTbf[d]): e.dma_start(out=t, in_=s),
              writes=(f"inTbf{d}",))
        p.dma("qg", "cast0", lambda e, s=flat2(outT[d]), t=flat2(outTbf[d]): e.dma_start(out=t, in_=s),
              writes=(f"outTbf{d}",))
        p.dma("qg", "cast0", lambda e, s=flat2(xpT[d]), t=flat2(xpTbf[d]): e.dma_start(out=t, in_=s),
              writes=(f"xpTbf{d}",))
        p.dma("qg", "cast0", lambda e, s=flat2(dtwT[d]), t=flat2(dtwTbf[d]): e.dma_start(out=t, in_=s),
              writes=(f"dtwTbf{d}",))
    p.dma("qg", "cast0", lambda e, s=flat2(mergeT2), t=flat2(mergeTbf): e.dma_start(out=t, in_=s),
          writes=("mergeTbf",))

    def flat2o(ap_tensor, off, n):
        return bass.AP(ap_tensor, off, [[n // 128, 128], [1, n // 128]])

    NW1 = 32 * 128 * 1024
    NW2 = 128 * 32768
    for j in range(NEXP):
        p.dma("qg", "cast1", lambda e, jj=j: e.dma_start(
            out=flat2o(w1bf.tensor, jj * NW1, NW1), in_=flat2o(w1p.tensor, jj * NW1, NW1)),
            writes=(f"w1bf{j}",))
        p.dma("qg", "cast1", lambda e, jj=j: e.dma_start(
            out=flat2o(w2bf.tensor, jj * NW2, NW2), in_=flat2o(w2blk.tensor, jj * NW2, NW2)),
            writes=(f"w2bf{j}",))

    p.dma("qx", "par", lambda e: e.dma_start(out=IDE[:], in_=ident), writes=("IDE",))
    for d in (0, 1):
        for src, off, wdt, nm in ((convw[d], PA_CONVW[d], 64, f"convw{d}"),
                                  (convb[d], PA_CONVB[d], 16, f"convb{d}"),
                                  (dtb[d], PA_DTB[d], 16, f"dtb{d}"),
                                  (Dparm[d], PA_D[d], 16, f"D{d}"),
                                  (Alog[d], PA_A[d], 256, f"A{d}")):
            p.dma("qx", "par", lambda e, s=src, o=off, w=wdt: e.dma_start(
                out=PARS[:, o:o + w], in_=s), writes=(nm,))
    p.dma("qx", "par", lambda e: e.dma_start(out=LOW[0:8, LW_B2:LW_B2 + 1024], in_=b2),
          writes=("b2f",))
    p.dma("qx", "par", lambda e: e.dma_start(
        out=PARS[:, PA_RB:PA_RB + 8], in_=bass.AP(rb.tensor, 0, [[0, 128], [1, 8]])),
        writes=("rbb",))
    p.dma("qx", "par", lambda e: e.dma_start(out=RWS[:], in_=rw), writes=("rw",))

    for d in (0, 1):
        Aap = PARS[:, PA_A[d]:PA_A[d] + 256]
        act(Aap, Aap, AF.Exp, 0.0, 1.0, reads=(f"A{d}",), writes=(f"A{d}",))
        p.op(VEC, lambda e, a=Aap: e.tensor_scalar_mul(a, a, -1.0),
             reads=(f"A{d}",), writes=(f"A{d}",))
    b2bfv = LOW[:, LW_B2BF:LW_B2BF + 512].bitcast(BF16)
    p.op(VEC, lambda e: e.tensor_copy(b2bfv[0:8, :], LOW[0:8, LW_B2:LW_B2 + 1024]),
         reads=("b2f",), writes=("b2bf",))

    if not pmask[0]:
        p.op(SCA, lambda e: e.nop(), reads=(), writes=())
        p.emit(nc, sems, qsems)
        stack.close()
        return nc
    stat = PARS[:, PA_STAT:PA_STAT + 96]
    for tt in range(16):
        xt = ar(P0_XT + tt * 1024, 1024)
        p.dma("qx", "xt", lambda e, t=xt, a=tt: e.dma_start(out=t, in_=xb[a * 128:(a + 1) * 128, :]),
              writes=(f"xt{tt}",))
        act(ar(P0_SC, 1024), xt, AF.Copy, 0.0, 1.0, reads=(f"xt{tt}",),
            writes=("p0sc", "stat_s"), accum=stat[:, tt:tt + 1])
        act(ar(P0_SC, 1024), xt, AF.Square, 0.0, 1.0, reads=(f"xt{tt}",),
            writes=("p0sc", "stat_s"), accum=stat[:, 16 + tt:17 + tt])
    p.op(VEC, lambda e: e.tensor_scalar_mul(stat[:, 32:48], stat[:, 0:16], 1.0 / DM),
         reads=("stat_s",), writes=("stat_m",), strict=True)
    p.op(VEC, lambda e: e.tensor_scalar_mul(stat[:, 48:64], stat[:, 16:32], 1.0 / DM),
         reads=("stat_s",), writes=("stat_e",), strict=True)
    act(stat[:, 64:80], stat[:, 32:48], AF.Square, 0.0, 1.0,
        reads=("stat_m",), writes=("stat_v",))
    p.op(VEC, lambda e: e.tensor_sub(stat[:, 64:80], stat[:, 48:64], stat[:, 64:80]),
         reads=("stat_e", "stat_v"), writes=("stat_v",), strict=True)
    p.op(VEC, lambda e: e.tensor_scalar_add(stat[:, 64:80], stat[:, 64:80], 1e-5),
         reads=("stat_v",), writes=("stat_v",), strict=True)
    act(stat[:, 64:80], stat[:, 64:80], AF.Sqrt, 0.0, 1.0, reads=("stat_v",), writes=("stat_v",))
    p.op(VEC, lambda e: e.reciprocal(stat[:, 80:96], stat[:, 64:80]),
         reads=("stat_v",), writes=("stat_r",), strict=True)

    logits = PARS[:, PA_LG:PA_LG + 128]
    gates = PARS[:, PA_G:PA_G + 128]
    for tt in range(16):
        xt = ar(P0_XT + tt * 1024, 1024)
        xnf = ar(P0_SC, 1024)
        p.op(VEC, lambda e, t=xt, o=xnf, c=tt: e.tensor_scalar(
            o, t, stat[:, 32 + c:33 + c], stat[:, 80 + c:81 + c], OP.subtract, OP.mult),
            reads=(f"xt{tt}", "stat_m", "stat_r"), writes=("p0sc",), strict=True)
        for dk in range(8):
            r = dk % 4
            pt = psr(0, r, 128)
            p.op(TEN, lambda e, o=pt, i=xnf[:, dk * 128:(dk + 1) * 128]:
                 e.transpose(o, i, IDE[:]),
                 reads=("p0sc", "IDE"), writes=prn(0, r))
            p.op(SCA, lambda e, o=XN[:, dk * 2048 + tt * 128: dk * 2048 + (tt + 1) * 128], i=pt:
                 e.activation(o, i, AF.Copy, bias=0.0, scale=1.0),
                 reads=prn(0, r), writes=("xn",))
            p.op(SCA, lambda e, o=ar(P0_PC + dk * 128, 128), i=pt:
                 e.activation(o, i, AF.Copy, bias=0.0, scale=1.0),
                 reads=prn(0, r), writes=(f"pc{dk}",))
        for dk in range(8):
            mm(PB[1][:, 0:8], ar(P0_PC + dk * 128, 128), RWS[:, dk * 8:(dk + 1) * 8],
               dk == 0, dk == 7, reads=(f"pc{dk}", "rw"), writes=prn(1, 0))
        p.op(VEC, lambda e, o=logits[:, tt * 8:(tt + 1) * 8]: e.tensor_add(
            o, PB[1][:, 0:8], PARS[:, PA_RB:PA_RB + 8]),
            reads=prn(1, 0) + ("rbb",), writes=(f"lg{tt}",), strict=True)
    sc = LOW[:, LW_SC:LW_SC + 128]
    for tt in range(16):
        lg = logits[:, tt * 8:(tt + 1) * 8]
        gt = gates[:, tt * 8:(tt + 1) * 8]
        m1, m2, ex, dn, iv, g2 = (sc[:, i:i + 1] for i in range(6))
        l2, mk1, mk2, t1 = sc[:, 8:16], sc[:, 16:24], sc[:, 24:32], sc[:, 32:40]
        p.op(VEC, lambda e, a=m1, b=lg: e.reduce_max(a, b, axis=AX.X),
             reads=(f"lg{tt}",), writes=("t2",), strict=True)
        p.op(VEC, lambda e, a=mk1, b=lg, c=m1: e.tensor_single_scalar(a, b, c, OP.is_ge),
             reads=(f"lg{tt}", "t2"), writes=("t2",), strict=True)
        p.op(VEC, lambda e, a=l2, b=mk1, c=lg: e.scalar_tensor_tensor(
            a, b, -1e30, c, OP.mult, OP.add), reads=(f"lg{tt}", "t2"), writes=("t2",), strict=True)
        p.op(VEC, lambda e, a=m2, b=l2: e.reduce_max(a, b, axis=AX.X),
             reads=("t2",), writes=("t2",), strict=True)
        p.op(VEC, lambda e, a=mk2, b=l2, c=m2: e.tensor_single_scalar(a, b, c, OP.is_ge),
             reads=("t2",), writes=("t2",), strict=True)
        p.op(VEC, lambda e, a=ex, b=m2, c=m1: e.tensor_sub(a, b, c),
             reads=("t2",), writes=("t2",), strict=True)
        act(ex, ex, AF.Exp, 0.0, 1.0, reads=("t2",), writes=("t2",), strict=True)
        p.op(VEC, lambda e, a=dn, b=ex: e.tensor_scalar_add(a, b, 1.0),
             reads=("t2",), writes=("t2",), strict=True)
        p.op(VEC, lambda e, a=iv, b=dn: e.reciprocal(a, b), reads=("t2",), writes=("t2",), strict=True)
        p.op(VEC, lambda e, a=g2, b=ex, c=iv: e.tensor_mul(a, b, c),
             reads=("t2",), writes=("t2",), strict=True)
        p.op(VEC, lambda e, a=t1, b=mk1, c=iv: e.tensor_single_scalar(a, b, c, OP.mult),
             reads=("t2",), writes=("t2",), strict=True)
        p.op(VEC, lambda e, a=gt, b=mk2, c=g2, d0=t1: e.scalar_tensor_tensor(
            a, b, c, d0, OP.mult, OP.add), reads=("t2",), writes=(f"g{tt}",), strict=True)

    p.fence(VEC, [f"xt{t}" for t in range(16)] + ["p0sc"] +
            [f"pc{d}" for d in range(8)] + ["xi", "xc", "sz", "de", "y", "hb"])

    def mamba_dir(d):
        p.dma("qs", "dtw", lambda e, s=dtwTbf[d]: e.dma_start(out=DTW[0:64, :], in_=s),
              reads=(f"dtwTbf{d}",), writes=("DTW",))
        p.dma("qs", "xpt", lambda e, s=xpTbf[d]: e.dma_start(out=XPT, in_=s),
              reads=(f"xpTbf{d}",), writes=("XPT",))

        for ch in range(NCH):
            if d == 0:
                t0 = ch * TC

                def xn_rhs(k, tb=t0):
                    return XN[:, k * 2048 + tb: k * 2048 + tb + TC]
            else:
                t_hi = 2047 - ch * TC

                def xn_rhs(k, th=t_hi):
                    return bass.AP(XN, k * 2048 + th, [[16384, 128], [-1, TC]])

            tailv = bass.AP(PARS, PA_TAIL, [[1664, 128], [3, 16], [1, 3]])
            if ch > 0:
                p.op(VEC, lambda e, tv=tailv: e.tensor_copy(
                    tv, ar3(O_XI, 259, 16, 3, inner_off=256)),
                    reads=("xi",), writes=("xtail",))
            for e2c in range(32):
                sl = WS_S[e2c % 2]
                p.dma("qs", f"wq{e2c % 2}", lambda e, ec=e2c, s=sl: e.dma_start(
                    out=WST[:, s:s + 1024], in_=inTbf[d][ec]),
                    reads=(f"inTbf{d}",), writes=(f"wq{e2c % 2}",))
                pi = 0
                r = e2c % 4
                for k in range(8):
                    mm(psr(pi, r), WST[:, sl + k * 128: sl + (k + 1) * 128], xn_rhs(k),
                       k == 0, k == 7,
                       reads=(f"wq{e2c % 2}", "xn"), writes=prn(pi, r))
                if r == 3:
                    rd = prn(pi, 0, 4)
                    srcb = bass.AP(PB[pi], 0, [[2048, 128], [512, 4], [1, 256]])
                    if e2c < 16:
                        dst = ar3(O_XI + (e2c - 3) * 259, 259, 4, 256, inner_off=3)
                        p.op(SCA, lambda e, o=dst, i=srcb:
                             e.activation(o, i, AF.Copy, bias=0.0, scale=1.0),
                             reads=rd, writes=("xi",))
                    else:
                        dst = ar3(O_SZ + (e2c - 16 - 3) * 256, 256, 4, 256)
                        p.op(SCA, lambda e, o=dst, i=srcb:
                             e.activation(o, i, AF.Silu, bias=0.0, scale=1.0),
                             reads=rd, writes=("sz",))
            if ch > 0:
                p.op(VEC, lambda e, tv=tailv: e.tensor_copy(
                    ar3(O_XI, 259, 16, 3), tv), reads=("xtail",), writes=("xi",), strict=True)
            else:
                p.op(VEC, lambda e: e.memset(ar3(O_XI, 259, 16, 3), 0.0),
                     reads=(), writes=("xi",))
            if DBG and d == 0 and ch == 0:
                dump(0, ar3(O_XI, 259, 16, 256, inner_off=3), ("xi",))
                dump(1, ar(O_SZ, 4096), ("sz",))
                p.dma("qx", "dbg", lambda e: e.dma_start(out=dbg2, in_=XN[:]),
                      reads=("xn",), writes=())
                p.dma("qx", "dbg", lambda e: e.dma_start(
                    out=dbg[5].bitcast(BF16)[:, 0:1024], in_=inTbf[0][0]),
                    reads=("inTbf0",), writes=())
                p.dma("qx", "dbg", lambda e: e.dma_start(
                    out=dbg[6].bitcast(BF16)[:, 0:1024], in_=inTbf[0][1]),
                    reads=("inTbf0",), writes=())
                dump(7, bass.AP(PARS, 0, [[1664, 128], [1, 1664]]),
                     ("stat_m", "stat_r", "stat_s"), w=1664)
            for ec in range(ECT):
                xcv = ar(O_XC + ec * 256, 256)
                cw = PARS[:, PA_CONVW[d] + ec * 4: PA_CONVW[d] + ec * 4 + 4]
                cb = PARS[:, PA_CONVB[d] + ec: PA_CONVB[d] + ec + 1]
                p.op(VEC, lambda e, o=xcv, i=ar(O_XI + ec * 259, 256), w=cw[:, 0:1], b=cb:
                     e.tensor_scalar(o, i, w, b, OP.mult, OP.add),
                     reads=("xi", f"convw{d}", f"convb{d}"), writes=("xc",), strict=True)
                for k in (1, 2, 3):
                    p.op(VEC, lambda e, o=xcv, i=ar(O_XI + ec * 259 + k, 256), w=cw[:, k:k + 1]:
                         e.scalar_tensor_tensor(o, i, w, o, OP.mult, OP.add),
                         reads=("xi", f"convw{d}", "xc"), writes=("xc",))
            p.op(SCA, lambda e: e.activation(ar(O_XC, 4096), ar(O_XC, 4096), AF.Silu,
                                             bias=0.0, scale=1.0),
                 reads=("xc",), writes=("xc",))
            if DBG and d == 0 and ch == 0:
                dump(2, ar(O_XC, 4096), ("xc",))
            xcbf = ar(O_XB, 2048).bitcast(BF16)
            p.op(VEC, lambda e, o=xcbf: e.tensor_copy(o, ar(O_XC, 4096)),
                 reads=("xc",), writes=("xbf",))
            for ec in range(ECT):
                mm(PB[0][0:96, 0:256], XPT[:, ec * 96:(ec + 1) * 96],
                   xcbf[:, ec * 256:(ec + 1) * 256],
                   ec == 0, ec == 15, reads=("XPT", "xbf"), writes=prn(0, 0))
            p.op(SCA, lambda e: e.activation(LOW[0:96, LW_DBL:LW_DBL + 256], PB[0][0:96, 0:256],
                                             AF.Copy, bias=0.0, scale=1.0),
                 reads=prn(0, 0), writes=("dbl",))
            par = ch % 2
            p.dma("qs", f"bcr{par}", lambda e, t=bcr[par]: e.dma_start(
                out=t, in_=LOW[64:96, LW_DBL:LW_DBL + 256]),
                reads=("dbl",), writes=(f"bcr{par}",))
            dtbf = LOW[:, LW_DTBF:LW_DTBF + 128].bitcast(BF16)
            p.op(VEC, lambda e, o=dtbf[0:64, :]: e.tensor_copy(o, LOW[0:64, LW_DBL:LW_DBL + 256]),
                 reads=("dbl",), writes=("dtbf",))
            for ec in range(ECT):
                r = ec % 4
                mm(psr(0, r), DTW[0:64, ec * 128:(ec + 1) * 128], dtbf[0:64, :],
                   True, True, reads=("DTW", "dtbf"), writes=prn(0, r))
                p.op(SCA, lambda e, o=ar(O_DE + ec * 256, 256), i=psr(0, r),
                     b=PARS[:, PA_DTB[d] + ec:PA_DTB[d] + ec + 1]:
                     e.activation(o, i, AF.Exp, bias=b, scale=1.0),
                     reads=prn(0, r) + (f"dtb{d}",), writes=("de",))
            p.op(SCA, lambda e: e.activation(ar(O_DE, 4096), ar(O_DE, 4096), AF.Ln,
                                             bias=1.0, scale=1.0),
                 reads=("de",), writes=("de",))
            if DBG and d == 0 and ch == 0:
                dump(3, bass.AP(LOW, 0, [[2048, 128], [0, 16], [1, 256]]), ("dbl",))
                dump(4, ar(O_DE, 4096), ("de",))
            for ec in range(ECT):
                p.op(VEC, lambda e, o=ar(O_Y + ec * 256, 256), i=ar(O_XC + ec * 256, 256),
                     s=PARS[:, PA_D[d] + ec:PA_D[d] + ec + 1]:
                     e.tensor_single_scalar(o, i, s, OP.mult),
                     reads=("xc", f"D{d}"), writes=("y",))
            p.op(VEC, lambda e: e.tensor_mul(ar(O_XC, 4096), ar(O_DE, 4096), ar(O_XC, 4096)),
                 reads=("de", "xc"), writes=("xc",))
            st = PARS[:, PA_ST[d]:PA_ST[d] + 256]
            for n in range(NST):
                bsl = n % 2
                bb = BCB[:, bsl * 512: bsl * 512 + 256]
                cc = BCB[:, bsl * 512 + 256: bsl * 512 + 512]
                p.dma("qs", f"bb{bsl}", lambda e, o=bb, t=bcr[par], r=n: e.dma_start(
                    out=o, in_=bass.AP(t.tensor, r * TC, [[0, 128], [1, TC]])),
                    reads=(f"bcr{par}",), writes=(f"bb{bsl}",))
                p.dma("qs", f"cb{bsl}", lambda e, o=cc, t=bcr[par], r=16 + n: e.dma_start(
                    out=o, in_=bass.AP(t.tensor, r * TC, [[0, 128], [1, TC]])),
                    reads=(f"bcr{par}",), writes=(f"cb{bsl}",))
                for eh in range(2):
                    dsl = (n * 2 + eh) % 2
                    hcb = ar(O_DBU + dsl * 2048, 2048)
                    p.op(POOL, lambda e, o=ar3(O_DBU + dsl * 2048, 256, 8, 256),
                         u0=ar3(O_XC + eh * 2048, 256, 8, 256),
                         b0=bass.AP(BCB, bsl * 512, [[1024, 128], [0, 8], [1, 256]]):
                         e.tensor_tensor(o, u0, b0, OP.mult),
                         reads=("xc", f"bb{bsl}"), writes=(f"dbu{dsl}",))
                    for ei in range(8):
                        ec = eh * 8 + ei
                        dasl = ec % 4
                        da = ar(O_DAS[dasl], 256)
                        p.op(SCA, lambda e, o=da, i=ar(O_DE + ec * 256, 256),
                             s=PARS[:, PA_A[d] + ec * 16 + n: PA_A[d] + ec * 16 + n + 1]:
                             e.activation(o, i, AF.Exp, bias=0.0, scale=s),
                             reads=("de", f"A{d}"), writes=(f"da{dasl}",))
                        hsl = ar(O_HB + (ec % 8) * 256, 256)
                        if ch == 0:
                            scrd = (f"da{dasl}", f"dbu{dsl}")
                            init = 0.0
                        else:
                            scrd = (f"da{dasl}", f"dbu{dsl}", "st")
                            init = st[:, ec * 16 + n: ec * 16 + n + 1]
                        p.op(VEC, lambda e, o=hsl, a=da,
                             b=ar(O_DBU + dsl * 2048 + ei * 256, 256), i0=init:
                             e.tensor_tensor_scan(o, a, b, i0, OP.mult, OP.add),
                             reads=scrd, writes=("hb",))
                        p.op(VEC, lambda e, o=hcb[:, ei * 256:(ei + 1) * 256], a=hsl, b=cc:
                             e.tensor_mul(o, a, b),
                             reads=("hb", f"cb{bsl}"), writes=(f"dbu{dsl}",))
                    if ch < NCH - 1:
                        cdst = bass.AP(PARS, PA_ST[d] + (eh * 8) * 16 + n, [[1664, 128], [16, 8]])
                        csrc = bass.AP(AR, O_HB + 255, [[AR_W, 128], [256, 8]])
                        p.op(SCA, lambda e, o=cdst, i=csrc:
                             e.activation(o, i, AF.Copy, bias=0.0, scale=1.0),
                             reads=("hb",), writes=("st",))
                    eng = POOL if n % 2 == 0 else VEC
                    p.op(eng, lambda e, o=ar(O_Y + eh * 2048, 2048), a=hcb:
                         e.tensor_tensor(o, a, o, OP.add),
                         reads=(f"dbu{dsl}", "y"), writes=("y",))
                    if INTERLEAVE:
                        moe_feed(4)
            if DBG and d == 0 and ch == 0:
                dump(5, ar(O_Y, 4096), ("y",))
                dump(6, ar(O_XC, 4096), ("xc",))
            ybf = ar(O_XB, 2048).bitcast(BF16)
            p.op(VEC, lambda e, o=ybf: e.tensor_tensor(o, ar(O_Y, 4096), ar(O_SZ, 4096), OP.mult),
                 reads=("y", "sz", "xbf"), writes=("xbf",))
            fbf3 = ar(O_FBF, 1024).bitcast(BF16).rearrange("q (a b) -> q a b", b=256)
            for d1c in range(8):
                sl = (d1c % 2) * 2048
                wn = (f"wq{(d1c % 2) * 2}", f"wq{(d1c % 2) * 2 + 1}")
                p.dma("qs", f"wo{d1c % 2}", lambda e, c=d1c, s=sl: e.dma_start(
                    out=WST[:, s:s + 2048], in_=outTbf[d][c]),
                    reads=(f"outTbf{d}",), writes=wn)
                r = d1c % 4
                for k in range(16):
                    mm(psr(0, r), WST[:, sl + k * 128: sl + (k + 1) * 128],
                       ybf[:, k * 256:(k + 1) * 256],
                       k == 0, k == 15, reads=wn + ("xbf",),
                       writes=prn(0, r))
                if r == 3:
                    wv = d1c // 4
                    half = fbf3[:, wv * 4:(wv + 1) * 4, :]
                    fdst = half if d == 0 else half[:, :, ::-1]
                    srcb = bass.AP(PB[0], 0, [[2048, 128], [512, 4], [1, 256]])
                    p.op(SCA, lambda e, o=fdst, i=srcb:
                         e.activation(o, i, AF.Copy, bias=0.0, scale=1.0),
                         reads=prn(0, 0, 4), writes=("fbf",))
            fflat = ar(O_FBF, 1024).bitcast(BF16)
            t_start = ch * TC if d == 0 else L - (ch + 1) * TC
            for k in range(8):
                sl = WS_S[k % 2]
                p.dma("qs", f"wq{k % 2}", lambda e, kk=k, s=sl: e.dma_start(
                    out=WST[:, s:s + 1024], in_=mergeTbf[d, kk]),
                    reads=("mergeTbf",), writes=(f"wq{k % 2}",))
                for ti in range(2):
                    for nh in range(2):
                        mm(PB[0][:, ti * 1024 + nh * 512: ti * 1024 + nh * 512 + 512],
                           fflat[:, k * 256 + ti * 128: k * 256 + ti * 128 + 128],
                           WST[:, sl + nh * 512: sl + (nh + 1) * 512],
                           k == 0, k == 7,
                           reads=(f"wq{k % 2}", "fbf"),
                           writes=prn(0, ti * 2 + nh))
            for ti in range(2):
                stg = ar(O_ST, 1024) if ti == 0 else ar(O_FBF, 1024)
                sw = (f"stg{ti}",) if ti == 0 else ("stg1", "fbf")
                p.op(SCA, lambda e, o=stg, i=PB[0][:, ti * 1024: ti * 1024 + 1024]:
                     e.activation(o, i, AF.Copy, bias=0.0, scale=1.0),
                     reads=prn(0, ti * 2, 2), writes=sw)
                p.dma("qx", f"at{d}", lambda e, s=stg, a=t_start + ti * 128:
                      e.dma_start(out=attn_d[d][a:a + 128, :], in_=s),
                      reads=sw, writes=(f"attn{d}",))

    do_moe = pmask[3] >= 1
    GT = LOW[:, LW_B2:LW_B2 + 1024].bitcast(BF16)
    UNITS = [(j, tq) for j in range(NEXP) for tq in range(NCH)] if do_moe else []
    unit_pos = [0]

    def moe_prepass():
        for tt in range(16 if do_moe else 0):
            pt = PB[0][0:8, (tt % 4) * 512:(tt % 4) * 512 + 128]
            p.op(TEN, lambda e, i=gates[:, tt * 8:(tt + 1) * 8], o=pt:
                 e.transpose(o, i, IDE[:]),
                 reads=(f"g{tt}", "IDE"), writes=prn(0, tt % 4))
            p.op(SCA, lambda e, o=GT[0:8, tt * 128:(tt + 1) * 128], i=pt:
                 e.activation(o, i, AF.Copy, bias=0.0, scale=1.0),
                 reads=prn(0, tt % 4),
                 writes=("gT",) if tt else ("gT", "b2f"))
        for tt in range(16 if do_moe else 0):
            for nh in range(2):
                mm(PB[1][:, nh * 512:(nh + 1) * 512], GT[0:8, tt * 128:(tt + 1) * 128],
                   b2bfv[0:8, nh * 512:(nh + 1) * 512],
                   True, True, reads=("gT", "b2bf"), writes=prn(1, nh))
            ys = tt % 2
            stg = ar(O_YSB + ys * 1024, 1024)
            p.op(SCA, lambda e, o=stg, i=PB[1][:, 0:1024]:
                 e.activation(o, i, AF.Copy, bias=0.0, scale=1.0),
                 reads=prn(1, 0, 2), writes=(f"ysb{ys}",))
            p.dma("qx", "fs", lambda e, s=stg, a=tt: e.dma_start(
                out=ffn_d[a * 128:(a + 1) * 128, :], in_=s),
                reads=(f"ysb{ys}",), writes=(f"ffn{tt}",))

    STEPS = [(j, tq, m) for j, tq in UNITS for m in range(32)]

    def w1load(g):
        if g >= len(STEPS):
            return
        j, tq, m = STEPS[g]
        ws = g % 2
        p.dma("qx", f"mw1{ws}", lambda e, jj=j, mi=m, s=ws: e.dma_start(
            out=MOEW[:, s * 1024:(s + 1) * 1024], in_=w1bf[jj, mi]),
            reads=(f"w1bf{j}",), writes=(f"w1p{ws}",), pipe=True)

    def w2load(g):
        j, tq, m = STEPS[g]
        ws = g % 2
        p.dma("qs", f"mw2{ws}", lambda e, jj=j, mi=m, s=ws: e.dma_start(
            out=MOEW[:, 2048 + s * 1024: 2048 + (s + 1) * 1024],
            in_=w2bf[jj][:, mi * 1024:(mi + 1) * 1024]),
            reads=(f"w2bf{j}",), writes=(f"w2p{ws}",), pipe=True)

    def emit_A(g):
        j, tq, m = STEPS[g]
        ws = g % 2
        r = m % 4
        for k8 in range(8):
            mm(psr(0, r), MOEW[:, ws * 1024 + k8 * 128: ws * 1024 + (k8 + 1) * 128],
               XN[:, k8 * 2048 + tq * TC: k8 * 2048 + tq * TC + TC],
               k8 == 0, k8 == 7, reads=(f"w1p{ws}", "xn"), writes=prn(0, r))

    def emit_relu(g):
        j, tq, m = STEPS[g]
        ws = g % 2
        r = m % 4
        p.op(SCA, lambda e, o=HP[:, ws * 256:(ws + 1) * 256], i=psr(0, r),
             b=PARS[:, PA_STAT + m:PA_STAT + m + 1]:
             e.activation(o, i, AF.Relu, bias=b, scale=1.0),
             reads=prn(0, r) + ("b1",), writes=(f"hp{ws}",))

    def emit_B(g):
        j, tq, m = STEPS[g]
        ws = g % 2
        for ti in range(2):
            for nh in range(2):
                mm(PB[1][:, ti * 1024 + nh * 512: ti * 1024 + nh * 512 + 512],
                   HP[:, ws * 256 + ti * 128: ws * 256 + ti * 128 + 128],
                   MOEW[:, 2048 + ws * 1024 + nh * 512:
                        2048 + ws * 1024 + (nh + 1) * 512],
                   m == 0, m == 31,
                   reads=(f"hp{ws}", f"w2p{ws}"), writes=prn(1, ti * 2 + nh))
        if m == 31:
            for ti in range(2):
                tt = tq * 2 + ti
                stg = ar(O_YSB + ti * 1024, 1024)
                gcol = gates[:, tt * 8 + j: tt * 8 + j + 1]
                p.op(SCA, lambda e, o=stg, i=PB[1][:, ti * 1024: ti * 1024 + 1024],
                     g=gcol: e.activation(o, i, AF.Copy, bias=0.0, scale=g),
                     reads=prn(1, ti * 2, 2) + (f"g{tt}",), writes=(f"ysb{ti}",))
                p.dma("qg", f"fa{ti}", lambda e, s=stg, a=tt: e.dma_start(
                    out=ffn_d[a * 128:(a + 1) * 128, :], in_=s, accum_op=OP.add),
                    reads=(f"ysb{ti}",) + (f"ffn{tt}",), writes=(f"ffn{tt}",))

    def moe_gen():
        for g in range(len(STEPS)):
            j, tq, m = STEPS[g]
            if g == 0:
                extra = ("stat_m", "stat_r", "stat_s", "stat_e", "stat_v")
                p.dma("qx", "b1", lambda e: e.dma_start(
                    out=PARS[:, PA_STAT:PA_STAT + 32], in_=b1blk[0]),
                    reads=(), writes=("b1",) + extra)
                w1load(0)
            w1load(g + 1)
            if g >= 2:
                emit_B(g - 2)
            w2load(g)
            emit_A(g)
            if g >= 1:
                emit_relu(g - 1)
            if tq == 0 and m == 0 and j > 0:
                p.dma("qx", "b1", lambda e, jj=j: e.dma_start(
                    out=PARS[:, PA_STAT:PA_STAT + 32], in_=b1blk[jj]),
                    reads=(), writes=("b1",))
            yield
        if STEPS:
            emit_relu(len(STEPS) - 1)
            emit_B(len(STEPS) - 2)
            emit_B(len(STEPS) - 1)

    MOE_IT = moe_gen()

    def moe_feed(k):
        for _ in range(k):
            if next(MOE_IT, "done") == "done":
                return

    moe_prepass()
    if pmask[1]:
        mamba_dir(0)
    if pmask[2]:
        mamba_dir(1)
    moe_feed(10 ** 9)

    p.fence(VEC, ["xi", "xc", "sz", "de", "y", "hb", "dbu0", "dbu1",
                  "da0", "da1", "xbf", "fbf", "stg0", "stg1", "xtail", "DTW", "XPT",
                  "w1p0", "w1p1", "w2p0", "w2p1", "hp0", "hp1", "ysb0", "ysb1",
                  "gT", "b1", "b2bf", "xn", "fence4"])

    for tt in range(16 if pmask[4] else 0):
        pp = tt % 2
        xt = ar(0 + pp * 1024, 1024)
        af_ = ar(8192 + pp * 1024, 1024)
        ab_ = ar(16384 + pp * 1024, 1024)
        ff_ = ar(24576 + pp * 1024, 1024)
        a0 = tt * 128
        p.dma("qx", "p4x", lambda e, o=xt, a=a0: e.dma_start(out=o, in_=xb[a:a + 128, :]),
              reads=("fence4",), writes=(f"p4x{pp}",))
        p.dma("qx", "p4a", lambda e, o=af_, a=a0: e.dma_start(out=o, in_=attn_d[0][a:a + 128, :]),
              reads=("attn0", "fence4"), writes=(f"p4a{pp}",))
        p.dma("qx", "p4b", lambda e, o=ab_, a=a0: e.dma_start(out=o, in_=attn_d[1][a:a + 128, :]),
              reads=("attn1", "fence4"), writes=(f"p4b{pp}",))
        p.dma("qx", "p4f", lambda e, o=ff_, a=a0: e.dma_start(out=o, in_=ffn_d[a:a + 128, :]),
              reads=(f"ffn{tt}", "fence4"), writes=(f"p4f{pp}",))
        p.op(VEC, lambda e, a=xt, b=af_: e.tensor_add(a, a, b),
             reads=(f"p4x{pp}", f"p4a{pp}"), writes=(f"p4x{pp}",))
        p.op(VEC, lambda e, a=xt, b=ab_: e.tensor_add(a, a, b),
             reads=(f"p4x{pp}", f"p4b{pp}"), writes=(f"p4x{pp}",))
        p.op(VEC, lambda e, a=xt, b=ff_: e.tensor_add(a, a, b),
             reads=(f"p4x{pp}", f"p4f{pp}"), writes=(f"p4x{pp}",))
        p.dma("qx", "out", lambda e, s=xt, a=a0: e.dma_start(out=out_h[a:a + 128, :], in_=s),
              reads=(f"p4x{pp}",), writes=("outd",))

    if pmask[4]:
        p.op(SCA, lambda e: e.nop(), reads=("outd",), writes=())

    p.emit(nc, sems, lambda nm: e_(nc.semaphore(nm)))
    stack.close()
    return nc


def prep_shared(inputs):

    shared = {}
    for d, pre in ((0, "fwd"), (1, "bwd")):
        c = "fb"[d]
        ip = np.asarray(inputs[f"L0_attn_{pre}_in_proj"], dtype=np.float32)
        shared[f"inT_{c}"] = np.ascontiguousarray(
            ip.T.reshape(8, 128, 32, 128).transpose(2, 1, 0, 3).reshape(32, 128, 1024))
        op_ = np.asarray(inputs[f"L0_attn_{pre}_out_proj"], dtype=np.float32)
        shared[f"outT_{c}"] = np.ascontiguousarray(
            op_.T.reshape(16, 128, 8, 128).transpose(2, 1, 0, 3).reshape(8, 128, 2048))
        xp = np.asarray(inputs[f"L0_attn_{pre}_x_proj"], dtype=np.float32)
        shared[f"xpT_{c}"] = np.ascontiguousarray(
            xp.T.reshape(16, 128, 96).transpose(1, 0, 2).reshape(128, 1536))
        shared[f"dtwT_{c}"] = np.ascontiguousarray(
            np.asarray(inputs[f"L0_attn_{pre}_dt_w"], dtype=np.float32).T)
        shared[f"convw_{c}"] = np.ascontiguousarray(
            np.asarray(inputs[f"L0_attn_{pre}_conv_w"], dtype=np.float32)
            .reshape(16, 128, 4).transpose(1, 0, 2).reshape(128, 64))
        shared[f"convb_{c}"] = np.ascontiguousarray(
            np.asarray(inputs[f"L0_attn_{pre}_conv_b"], dtype=np.float32).reshape(16, 128).T)
        shared[f"dtb_{c}"] = np.ascontiguousarray(
            np.asarray(inputs[f"L0_attn_{pre}_dt_b"], dtype=np.float32).reshape(16, 128).T)
        shared[f"D_{c}"] = np.ascontiguousarray(
            np.asarray(inputs[f"L0_attn_{pre}_D"], dtype=np.float32).reshape(16, 128).T)
        shared[f"A_{c}"] = np.ascontiguousarray(
            np.asarray(inputs[f"L0_attn_{pre}_A_log"], dtype=np.float32)
            .reshape(16, 128, 16).transpose(1, 0, 2).reshape(128, 256))
    mw = np.asarray(inputs["L0_attn_merge_w"], dtype=np.float32)
    shared["mergeT2"] = np.ascontiguousarray(mw.T.reshape(2, 8, 128, 1024))
    shared["rw"] = np.ascontiguousarray(
        np.asarray(inputs["L0_ffn_router_w"], dtype=np.float32)
        .reshape(8, 128, 8).transpose(1, 0, 2).reshape(128, 64))
    shared["rb"] = np.ascontiguousarray(
        np.asarray(inputs["L0_ffn_router_b"], dtype=np.float32).reshape(1, 8))
    w1 = np.asarray(inputs["L0_ffn_w1"], dtype=np.float32)
    shared["w1p"] = np.ascontiguousarray(
        w1.reshape(8, 8, 128, 32, 128).transpose(0, 3, 2, 1, 4).reshape(8, 32, 128, 1024))
    w2 = np.asarray(inputs["L0_ffn_w2"], dtype=np.float32)
    shared["w2blk"] = np.ascontiguousarray(
        w2.reshape(8, 32, 128, 1024).transpose(0, 2, 1, 3).reshape(8, 128, 32768))
    shared["b1blk"] = np.ascontiguousarray(
        np.asarray(inputs["L0_ffn_b1"], dtype=np.float32).reshape(8, 32, 128).transpose(0, 2, 1))
    shared["b2"] = np.ascontiguousarray(np.asarray(inputs["L0_ffn_b2"], dtype=np.float32))
    shared["ident"] = np.eye(128, dtype=np.float32)
    return shared


def kernel(**inputs):
    x = np.ascontiguousarray(np.asarray(inputs["L0_attn_x"], dtype=np.float32))
    shared = prep_shared(inputs)
    nc = build_nc()
    in_maps = []
    for b in range(8):
        m = dict(shared)
        m["xb"] = np.ascontiguousarray(x[b])
        in_maps.append(m)
    res = run_bass_kernel_spmd(nc, in_maps, core_ids=list(range(8)))
    out = np.stack([np.asarray(r["out"]) for r in res.results], axis=0)
    return out.astype(np.float32)


if __name__ == "__main__":
    nc = build_nc()
    print("build ok:", nc.n_instructions(), "instructions")
